# version 30
# baseline (speedup 1.0000x reference)
import numpy as np
import ml_dtypes
from contextlib import ExitStack

import concourse.bass as bass
import concourse.mybir as mybir
from concourse.bass_utils import run_bass_kernel_spmd
from concourse import library_config

F32 = mybir.dt.float32
BF16 = mybir.dt.bfloat16
AF = mybir.ActivationFunctionType
ALU = mybir.AluOpType
BF = ml_dtypes.bfloat16

D = 1024
HC = 8
INTER = 4096
IC = 32
EPS = 1e-5


def build_nc(SEQ=4096):
    QN = SEQ // 2
    G = SEQ // 256
    GQ = G // 2
    TILES = QN // 128
    WIN = SEQ // 512
    TC = SEQ // 128
    PB = TC // 8
    FS = QN // 256

    nc = bass.Bass("TRN2", target_bir_lowering=False)
    xb = nc.dram_tensor("xb", [SEQ, D], F32, kind="ExternalInput")
    Wq = nc.dram_tensor("Wq", [D, D], F32, kind="ExternalInput")
    Wk = nc.dram_tensor("Wk", [D, D], F32, kind="ExternalInput")
    Wv = nc.dram_tensor("Wv", [D, D], F32, kind="ExternalInput")
    Wo = nc.dram_tensor("Wo", [D, D], F32, kind="ExternalInput")
    bq = nc.dram_tensor("bq", [D], F32, kind="ExternalInput")
    bk = nc.dram_tensor("bk", [D], F32, kind="ExternalInput")
    bv = nc.dram_tensor("bv", [D], F32, kind="ExternalInput")
    bo = nc.dram_tensor("bo", [D], F32, kind="ExternalInput")
    W1 = nc.dram_tensor("W1", [D, INTER], F32, kind="ExternalInput")
    b1 = nc.dram_tensor("b1", [INTER], F32, kind="ExternalInput")
    W2 = nc.dram_tensor("W2", [INTER, D], F32, kind="ExternalInput")
    b2 = nc.dram_tensor("b2", [D], F32, kind="ExternalInput")
    eye = nc.dram_tensor("eye", [128, 128], BF16, kind="ExternalInput")
    ones = nc.dram_tensor("ones", [1, 128], BF16, kind="ExternalInput")
    y = nc.dram_tensor("y", [QN, D], F32, kind="ExternalOutput")
    w1bf = nc.dram_tensor("w1bf", [D, INTER], BF16, kind="Internal")
    w2bf = nc.dram_tensor("w2bf", [INTER, D], BF16, kind="Internal")
    on_scr = nc.dram_tensor("on_scr", [QN, D], BF16, kind="Internal")
    NVQ0 = min(4, SEQ // 512)
    KQ_B0 = [((SEQ // 512) * (q + 1)) // NVQ0 for q in range(NVQ0)]
    kxq_in = [nc.dram_tensor(f"kxq_in{i}", [1, 128, HC * (KQ_B0[i] - (KQ_B0[i - 1] if i else 0)) * 256], BF16, kind="Internal") for i in range(NVQ0)]
    kxq_out = [nc.dram_tensor(f"kxq_out{i}", [2, 128, HC * (KQ_B0[i] - (KQ_B0[i - 1] if i else 0)) * 256], BF16, kind="Internal") for i in range(NVQ0)]
    QCOLS = ((SEQ // 512) // NVQ0) * 2 * D
    vx_in = [nc.dram_tensor(f"vx_in{i}", [1, 128, QCOLS], BF16, kind="Internal") for i in range(NVQ0)]
    vx_out = [nc.dram_tensor(f"vx_out{i}", [2, 128, QCOLS], BF16, kind="Internal") for i in range(NVQ0)]

    t_eng = nc.tensor
    s_eng = nc.scalar
    v_eng = nc.vector
    g_eng = nc.gpsimd
    q_eng = nc.sync

    top = ExitStack()
    E = top.enter_context

    gdma = E(nc.semaphore("gdma"))
    gdmas = [gdma] + [E(nc.semaphore(f"gdma{i}")) for i in range(1, 4)]
    ldc = E(nc.semaphore("ldc"))
    ldx0 = E(nc.semaphore("ldx0"))
    ldx1 = E(nc.semaphore("ldx1"))
    ldx2 = E(nc.semaphore("ldx2"))
    smt = E(nc.semaphore("smt"))
    sms = E(nc.semaphore("sms"))
    smv = E(nc.semaphore("smv"))
    stOn0 = E(nc.semaphore("stOn0"))
    stOn1 = E(nc.semaphore("stOn1"))
    lOn = E(nc.semaphore("lOn"))
    lxr = E(nc.semaphore("lxr"))
    sty = E(nc.semaphore("sty"))
    stw = E(nc.semaphore("stw"))
    ldwo = E(nc.semaphore("ldwo"))
    ldw1 = E(nc.semaphore("ldw1"))
    ldw2 = E(nc.semaphore("ldw2"))
    ldpt = E(nc.semaphore("ldpt"))
    ccs = E(nc.semaphore("ccs"))
    stxk = [E(nc.semaphore(f"stxk{i}")) for i in range(4)]
    stxv = [E(nc.semaphore(f"stxv{i}")) for i in range(4)]

    C = dict(gd=0, lc=0, lx0=0, lx1=0, lx2=0, t=0, s=0, v=0, so0=0, so1=0,
             lon=0, lxr=0, sty=0)

    def tinc(ins):
        ins.then_inc(smt, 1); C["t"] += 1; return C["t"]

    def sinc(ins):
        ins.then_inc(sms, 1); C["s"] += 1; return C["s"]

    def vinc(ins):
        ins.then_inc(smv, 1); C["v"] += 1; return C["v"]

    GDC = [0, 0, 0, 0]
    GDN = [0]

    def gdma_start(out, in_):
        i = GDN[0] % 4
        GDN[0] += 1
        if GDC[i] > 0:
            g_eng.wait_ge(gdmas[i], GDC[i])
        g_eng.dma_start(out=out, in_=in_).then_inc(gdmas[i], 16)
        GDC[i] += 16
        return (gdmas[i], GDC[i])

    eyesb = E(nc.sbuf_tensor("eyesb", [128, 128], BF16))
    ones_sb = E(nc.sbuf_tensor("ones_sb", [1, 128], BF16))
    bq_sb = E(nc.sbuf_tensor("bq_sb", [128, HC], F32))
    bk_sb = E(nc.sbuf_tensor("bk_sb", [128, HC], F32))
    b1_sb = E(nc.sbuf_tensor("b1_sb", [128, IC], F32))
    bv_sb = E(nc.sbuf_tensor("bv_sb", [1, D], BF16))
    bo_sb = E(nc.sbuf_tensor("bo_sb", [1, D], BF16))
    b2_sb = E(nc.sbuf_tensor("b2_sb", [1, D], BF16))
    stats2 = [E(nc.sbuf_tensor(f"stats{i}", [128, 12], F32)) for i in range(2)]
    mvt2 = [E(nc.sbuf_tensor(f"mvt{i}", [128, 2], F32)) for i in range(2)]
    rstd2 = [E(nc.sbuf_tensor(f"rstd{i}", [128, 1], F32)) for i in range(2)]
    nmr2 = [E(nc.sbuf_tensor(f"nmr{i}", [128, 1], F32)) for i in range(2)]
    mrow = E(nc.sbuf_tensor("mrow", [128, 1], F32))
    negm = E(nc.sbuf_tensor("negm", [128, 1], F32))
    lrow = E(nc.sbuf_tensor("lrow", [128, 1], F32))
    rrow = E(nc.sbuf_tensor("rrow", [128, 1], F32))
    mcol = E(nc.sbuf_tensor("mcol", [128, 16], F32))

    q_eng.dma_start(out=eyesb[:], in_=eye[:, :]).then_inc(ldc, 16); C["lc"] += 16
    q_eng.dma_start(out=ones_sb[:], in_=ones[:, :]).then_inc(ldc, 16); C["lc"] += 16
    q_eng.dma_start(out=bq_sb[:], in_=bq.rearrange("(e p) -> p e", p=128), allow_slow_non_contiguous=True).then_inc(ldc, 16); C["lc"] += 16
    q_eng.dma_start(out=bk_sb[:], in_=bk.rearrange("(e p) -> p e", p=128), allow_slow_non_contiguous=True).then_inc(ldc, 16); C["lc"] += 16
    q_eng.dma_start(out=b1_sb[:], in_=b1.rearrange("(i p) -> p i", p=128), allow_slow_non_contiguous=True).then_inc(ldc, 16); C["lc"] += 16
    LC_TOTAL = C["lc"]

    GD_BIASES = [gdma_start(bv_sb[:], bv[None, :]),
                 gdma_start(bo_sb[:], bo[None, :]),
                 gdma_start(b2_sb[:], b2[None, :])]

    def emit_w_transpose(wtmp_sb, WT_sb, gd_done, pss, base=0, prev_ck=0):
        evac_ck = {}
        nb = len(pss)
        for ec in range(HC):
            if ec == 0:
                t_eng.wait_ge(gd_done[0], gd_done[1])
                t_eng.wait_ge(ldc, LC_TOTAL)
                if prev_ck:
                    t_eng.wait_ge(sms, prev_ck)
            if ec >= nb:
                t_eng.wait_ge(sms, evac_ck[ec - nb])
            ins = None
            for hcc in range(HC):
                ins = t_eng.transpose(pss[ec % nb][:, hcc * 128:(hcc + 1) * 128],
                                      wtmp_sb[:, base + ec * D + hcc * 128: base + ec * D + (hcc + 1) * 128],
                                      eyesb[:])
            tk = tinc(ins)
            s_eng.wait_ge(smt, tk)
            cp = s_eng.copy(WT_sb.rearrange("p (h e) -> p h e", h=HC)[:, :, ec * 128:(ec + 1) * 128],
                            pss[ec % nb].rearrange("p (h j) -> p h j", h=HC))
            evac_ck[ec] = sinc(cp)
        return evac_ck[HC - 1]

    def emit_ln_chunk(xg_sb, col0, xn_sb, prev2_apply_ck, xn_free_t_ck,
                      ld_sem=None, ld_val=0, par=0):
        stats, mvt, rstd, nmr = stats2[par], mvt2[par], rstd2[par], nmr2[par]
        if ld_sem is not None:
            v_eng.wait_ge(ld_sem, ld_val)
        if prev2_apply_ck:
            v_eng.wait_ge(sms, prev2_apply_ck)
        v_eng.bn_stats(stats[:, 0:6], xg_sb[:, col0:col0 + 512])
        v_eng.bn_stats(stats[:, 6:12], xg_sb[:, col0 + 512:col0 + D])
        v_eng.drain()
        v_eng.bn_aggr(mvt[:], stats[:])
        v_eng.drain()
        v_eng.tensor_scalar(rstd[:], mvt[:, 1:2], EPS, None, ALU.add)
        v_eng.drain()
        v_eng.reciprocal(rstd[:], rstd[:])
        ts = v_eng.tensor_scalar(nmr[:], mvt[:, 0:1], -1.0, None, ALU.mult)
        v1 = vinc(ts)
        s_eng.wait_ge(smv, v1)
        s_eng.activation(rstd[:], rstd[:], AF.Sqrt)
        s_eng.drain()
        s_eng.activation(nmr[:], nmr[:], AF.Identity, scale=rstd[:])
        s_eng.drain()
        if xn_free_t_ck:
            s_eng.wait_ge(smt, xn_free_t_ck)
        ap = s_eng.activation(xn_sb[:], xg_sb[:, col0:col0 + D], AF.Identity,
                              bias=nmr[:], scale=rstd[:])
        return sinc(ap)

    PAIR_GROUPS = [[0, 1], [2, 3], [4, 5], [6, 7]]
    CC = dict(ccs=0)
    KL_DONE = []
    VL_DONE = []

    def dedup_waits(tups):
        m = {}
        for sem, val in tups:
            if id(sem) not in m or m[id(sem)][1] < val:
                m[id(sem)] = (sem, val)
        return list(m.values())

    ln_last = {0: 0, 1: 0}
    ln_k = [0]

    def ln_chunk(xg_sb, col0, xn_sb, xn_free_t_ck, ld_sem=None, ld_val=0):
        par = ln_k[0] % 2
        ck = emit_ln_chunk(xg_sb, col0, xn_sb, ln_last[par], xn_free_t_ck,
                           ld_sem, ld_val, par=par)
        ln_last[par] = ck
        ln_k[0] += 1
        return ck

    s_kq = ExitStack()
    K_sb = s_kq.enter_context(nc.sbuf_tensor("K_sb", [128, HC * SEQ], BF16))
    Q_sb = s_kq.enter_context(nc.sbuf_tensor("Q_sb", [128, HC * QN], BF16))
    if HC * QN >= HC * D:
        wv_stage, wv_base = K_sb, HC * QN
    else:
        wv_stage = s_kq.enter_context(nc.sbuf_tensor("wvstg", [128, HC * D], BF16))
        wv_base = 0

    s_swa = ExitStack()
    WqT = s_swa.enter_context(nc.sbuf_tensor("WqT", [128, HC * D], BF16))
    WkT = s_swa.enter_context(nc.sbuf_tensor("WkT", [128, HC * D], BF16))
    psTa = s_swa.enter_context(nc.psum_tensor("psTa", [128, 1024], BF16))
    psTb = s_swa.enter_context(nc.psum_tensor("psTb", [128, 1024], BF16))
    psTa2 = s_swa.enter_context(nc.psum_tensor("psTa2", [128, 1024], BF16))
    psTb2 = s_swa.enter_context(nc.psum_tensor("psTb2", [128, 1024], BF16))

    s_wtmp = ExitStack()
    wtmpA = s_wtmp.enter_context(nc.sbuf_tensor("wtmpA", [128, HC * D], BF16))
    wtmpA2 = s_wtmp.enter_context(nc.sbuf_tensor("wtmpA2", [128, HC * D], BF16))
    gq = gdma_start(wtmpA.rearrange("p (e d) -> p e d", e=HC),
                    Wq.rearrange("(e p) d -> p e d", p=128))
    g_eng.wait_ge(gq[0], gq[1])
    gk = gdma_start(wtmpA2.rearrange("p (e d) -> p e d", e=HC),
                    Wk.rearrange("(e p) d -> p e d", p=128))
    g_eng.wait_ge(gk[0], gk[1])
    gv = gdma_start(wv_stage[:, wv_base:wv_base + HC * D].rearrange("p (e d) -> p e d", e=HC),
                    Wv.rearrange("(e p) d -> p e d", p=128))
    g_eng.wait_ge(gv[0], gv[1])
    w1_pre, w2_pre = [], []
    pre_pieces = []
    for r in range(0, D, 64):
        pre_pieces.append((w1_pre, w1bf[r:r + 64, :], W1[r:r + 64, :]))
    for r in range(0, INTER, 256):
        pre_pieces.append((w2_pre, w2bf[r:r + 256, :], W2[r:r + 256, :]))
    pp_i = [0]

    def emit_pre_pieces(n):
        while n > 0 and pp_i[0] < len(pre_pieces):
            lst, dst, srcp = pre_pieces[pp_i[0]]
            lst.append(gdma_start(dst, srcp))
            pp_i[0] += 1
            n -= 1

    emit_pre_pieces(8)
    wq_ck = emit_w_transpose(wtmpA, WqT, gq, [psTa, psTb, psTa2, psTb2])
    wk_ck = emit_w_transpose(wtmpA2, WkT, gk, [psTa, psTb, psTa2, psTb2],
                             prev_ck=wq_ck)

    s_wtmp.close()

    s_xga = ExitStack()
    xgA0 = s_xga.enter_context(nc.sbuf_tensor("xgA0", [128, 2 * D], F32))
    xgA1 = s_xga.enter_context(nc.sbuf_tensor("xgA1", [128, 2 * D], F32))
    xnA2 = [s_xga.enter_context(nc.sbuf_tensor(f"xnA{i}", [128, D], BF16)) for i in range(2)]
    xnTA2 = [s_xga.enter_context(nc.sbuf_tensor(f"xnTA{i}", [128, HC * 256], BF16)) for i in range(2)]
    psK0 = s_xga.enter_context(nc.psum_tensor("psK0", [128, 512], F32))
    psK1 = s_xga.enter_context(nc.psum_tensor("psK1", [128, 512], F32))
    psQ0 = s_xga.enter_context(nc.psum_tensor("psQ0", [128, 512], F32))
    psQ1 = s_xga.enter_context(nc.psum_tensor("psQ1", [128, 512], F32))

    xgs = [xgA0, xgA1]
    ldxs = [ldx0, ldx1]
    lxk = ["lx0", "lx1"]
    xv = xb.rearrange("(g c p) d -> g p c d", p=128, c=2)

    st_a = dict(xgfree={}, lx_at={})

    def emit_xload(g, state):
        par = g % 2
        if g >= 2:
            q_eng.wait_ge(sms, state["xgfree"][g - 2])
        q_eng.dma_start(out=xgs[par].rearrange("p (c d) -> p c d", c=2),
                        in_=xv[g]).then_inc(ldxs[par], 16)
        C[lxk[par]] += 16
        state["lx_at"][g] = C[lxk[par]]

    q_eng.wait_ge(sms, wq_ck)
    emit_xload(0, st_a)
    if G > 1:
        emit_xload(1, st_a)

    apply_ck = 0
    xnt_evac = {}
    xnt_trans = {}
    kq_mm_done = {}
    kevac = {}
    qevac = {}

    def emit_prep_A(g):
        par = g % 2
        apls = {}
        for cc in range(2):
            k = 2 * g + cc
            apls[cc] = ln_chunk(xgs[par], cc * D, xnA2[k % 2],
                                xnt_trans.get(k - 2), ldxs[par], st_a["lx_at"][g])
            if cc == 1:
                st_a["xgfree"][g] = apls[cc]
                if g + 2 < G // 2:
                    emit_xload(g + 2, st_a)

        for cc in range(2):
            k = 2 * g + cc
            t_eng.wait_ge(sms, apls[cc])
            if k >= 4:
                t_eng.wait_ge(sms, xnt_evac[k - 4])
            ins = None
            pst = [psTa, psTb, psTa2, psTb2][k % 4]
            for hcc in range(HC):
                ins = t_eng.transpose(pst[:, hcc * 128:(hcc + 1) * 128],
                                      xnA2[k % 2][:, hcc * 128:(hcc + 1) * 128], eyesb[:])
            xnt_trans[k] = tinc(ins)
            s_eng.wait_ge(smt, xnt_trans[k])
            if g >= 2 and cc == 0:
                s_eng.wait_ge(smt, kq_mm_done[g - 2])
            cp = s_eng.copy(
                xnTA2[g % 2].rearrange("p (h s) -> p h s", h=HC)[:, :, cc * 128:(cc + 1) * 128],
                pst.rearrange("p (h j) -> p h j", h=HC))
            xnt_evac[k] = sinc(cp)

    G2 = G // 2
    emit_prep_A(0)
    for g in range(G2):
        if g + 1 < G2:
            emit_prep_A(g + 1)
        xnTA = xnTA2[g % 2]
        t_eng.wait_ge(sms, xnt_evac[2 * g + 1])
        if g == 0:
            t_eng.wait_ge(sms, wk_ck)
        psKs = [psK0, psK1]
        for ecp in range(HC // 2):
            if (g, ecp - 2) in kevac:
                t_eng.wait_ge(sms, kevac[(g, ecp - 2)])
            elif g >= 1 and ecp < 2:
                t_eng.wait_ge(sms, kevac[(g - 1, HC // 2 - 2 + ecp)])
            ins = None
            for sub in range(2):
                ec = 2 * ecp + sub
                for hcc in range(HC):
                    ins = t_eng.matmul(psKs[ecp % 2][:, sub * 256:(sub + 1) * 256],
                                       WkT[:, hcc * D + ec * 128: hcc * D + (ec + 1) * 128],
                                       xnTA[:, hcc * 256:(hcc + 1) * 256],
                                       start=(hcc == 0), stop=(hcc == HC - 1))
            tk = tinc(ins)
            s_eng.wait_ge(smt, tk)
            ev = None
            for sub in range(2):
                ec = 2 * ecp + sub
                ev = s_eng.activation(K_sb[:, ec * QN + g * 256: ec * QN + (g + 1) * 256],
                                      psKs[ecp % 2][:, sub * 256:(sub + 1) * 256], AF.Identity,
                                      bias=bk_sb[:, ec:ec + 1], scale=1.0)
            kevac[(g, ecp)] = sinc(ev)
        if True:
            if g == 0:
                t_eng.wait_ge(sms, wq_ck)
            psQs = [psQ0, psQ1]
            for ecp in range(HC // 2):
                if (g, ecp - 2) in qevac:
                    t_eng.wait_ge(sms, qevac[(g, ecp - 2)])
                elif g >= 1 and ecp < 2:
                    t_eng.wait_ge(sms, qevac[(g - 1, HC // 2 - 2 + ecp)])
                ins = None
                for sub in range(2):
                    ec = 2 * ecp + sub
                    for hcc in range(HC):
                        ins = t_eng.matmul(psQs[ecp % 2][:, sub * 256:(sub + 1) * 256],
                                           WqT[:, hcc * D + ec * 128: hcc * D + (ec + 1) * 128],
                                           xnTA[:, hcc * 256:(hcc + 1) * 256],
                                           start=(hcc == 0), stop=(hcc == HC - 1))
                tk = tinc(ins)
                s_eng.wait_ge(smt, tk)
                ev = None
                for sub in range(2):
                    ec = 2 * ecp + sub
                    ev = s_eng.activation(Q_sb[:, ec * QN + g * 256: ec * QN + (g + 1) * 256],
                                          psQs[ecp % 2][:, sub * 256:(sub + 1) * 256], AF.Identity,
                                          bias=bq_sb[:, ec:ec + 1], scale=1.0)
                qevac[(g, ecp)] = sinc(ev)
        kq_mm_done[g] = C["t"]
        if (g + 1) in KQ_B0:
            qi = KQ_B0.index(g + 1)
            lo, hi = (KQ_B0[qi - 1] if qi else 0), g + 1
            q_eng.wait_ge(sms, kevac[(g, HC // 2 - 1)])
            q_eng.dma_start(
                out=kxq_in[qi][0],
                in_=K_sb[:, 0:HC * QN].rearrange("p (e q) -> p e q", e=HC)[:, :, lo * 256:hi * 256],
            ).then_inc(stxk[qi], 16)
            g_eng.wait_ge(stxk[qi], 16)
            g_eng.collective_compute("AllGather", ALU.bypass,
                                     replica_groups=PAIR_GROUPS,
                                     ins=[kxq_in[qi][:]],
                                     outs=[kxq_out[qi][:]]).then_inc(ccs, 1)
            CC["ccs"] += 1
            emit_pre_pieces(8)

    emit_pre_pieces(99)
    sweepA_s_ck = C["s"]
    s_xga.close()
    s_swa.close()

    s_v = ExitStack()
    V_sb = s_v.enter_context(nc.sbuf_tensor("V_sb", [128, TC * D], BF16))

    s_swb = ExitStack()
    WvT = s_swb.enter_context(nc.sbuf_tensor("WvT", [128, HC * D], BF16))
    psTc = s_swb.enter_context(nc.psum_tensor("psTc", [128, 1024], BF16))
    psTd = s_swb.enter_context(nc.psum_tensor("psTd", [128, 1024], BF16))
    psTc2 = s_swb.enter_context(nc.psum_tensor("psTc2", [128, 1024], BF16))
    psTd2 = s_swb.enter_context(nc.psum_tensor("psTd2", [128, 1024], BF16))

    wv_ck = emit_w_transpose(wv_stage, WvT, gv, [psTc, psTd, psTc2, psTd2],
                             base=wv_base)

    g_eng.wait_ge(ccs, CC["ccs"])
    g_eng.wait_ge(sms, wv_ck)
    for qi in range(len(KQ_B0)):
        lo, hi = (KQ_B0[qi - 1] if qi else 0), KQ_B0[qi]
        for r in range(2):
            KL_DONE.append(gdma_start(
                K_sb[:, r * HC * QN:(r + 1) * HC * QN]
                    .rearrange("p (e q) -> p e q", e=HC)[:, :, lo * 256:hi * 256],
                kxq_out[qi][r]))

    s_xgb = ExitStack()
    xc3 = [s_xgb.enter_context(nc.sbuf_tensor(f"xcB{i}", [128, D], F32)) for i in range(3)]
    xnB2 = [s_xgb.enter_context(nc.sbuf_tensor(f"xnB{i}", [128, D], BF16)) for i in range(2)]
    xnTB2 = [s_xgb.enter_context(nc.sbuf_tensor(f"xnTB{i}", [128, HC * 256], BF16)) for i in range(2)]
    psV0 = s_xgb.enter_context(nc.psum_tensor("psV0", [128, 512], F32))
    psV1 = s_xgb.enter_context(nc.psum_tensor("psV1", [128, 512], F32))

    st_b = dict(apply={}, lx_at={})
    xnt_evac = {}
    xnt_trans = {}
    v_mm_done = {}
    vevac = {}
    xvc = xb.rearrange("(k p) d -> k p d", p=128)

    ldx3 = [ldx0, ldx1, ldx2]
    lxk3 = ["lx0", "lx1", "lx2"]

    def emit_xload_B(k):
        if k >= 3:
            q_eng.wait_ge(sms, st_b["apply"][k - 3])
        q_eng.dma_start(out=xc3[k % 3][:], in_=xvc[k]).then_inc(ldx3[k % 3], 16)
        C[lxk3[k % 3]] += 16
        st_b["lx_at"][k] = C[lxk3[k % 3]]

    q_eng.wait_ge(sms, wv_ck)
    for k in range(min(3, G)):
        emit_xload_B(k)

    def emit_prep_B(g):
        apls = {}
        for cc in range(2):
            k = 2 * g + cc
            apls[cc] = ln_chunk(xc3[k % 3], 0, xnB2[k % 2],
                                xnt_trans.get(k - 2), ldx3[k % 3], st_b["lx_at"][k])
            st_b["apply"][k] = apls[cc]
            if k + 3 < G:
                emit_xload_B(k + 3)
        for cc in range(2):
            k = 2 * g + cc
            t_eng.wait_ge(sms, apls[cc])
            if k >= 4:
                t_eng.wait_ge(sms, xnt_evac[k - 4])
            ins = None
            pst = [psTc, psTd, psTc2, psTd2][k % 4]
            for hcc in range(HC):
                ins = t_eng.transpose(pst[:, hcc * 128:(hcc + 1) * 128],
                                      xnB2[k % 2][:, hcc * 128:(hcc + 1) * 128], eyesb[:])
            xnt_trans[k] = tinc(ins)
            s_eng.wait_ge(smt, xnt_trans[k])
            if g >= 2 and cc == 0:
                s_eng.wait_ge(smt, v_mm_done[g - 2])
            cp = s_eng.copy(
                xnTB2[g % 2].rearrange("p (h s) -> p h s", h=HC)[:, :, cc * 128:(cc + 1) * 128],
                pst.rearrange("p (h j) -> p h j", h=HC))
            xnt_evac[k] = sinc(cp)

    NVQ = min(4, G // 2)
    VQ_BOUNDS = [((G // 2) * (q + 1)) // NVQ for q in range(NVQ)]

    emit_prep_B(0)
    for g in range(G // 2):
        if g + 1 < G // 2:
            emit_prep_B(g + 1)
        xnTB = xnTB2[g % 2]
        t_eng.wait_ge(sms, xnt_evac[2 * g + 1])
        if g == 0:
            t_eng.wait_ge(sms, wv_ck)
            for _sem, _val in GD_BIASES:
                t_eng.wait_ge(_sem, _val)
            t_eng.wait_ge(ldc, LC_TOTAL)
        psVs = [psV0, psV1]
        for cc in range(2):
            tci = 2 * g + cc
            for half in range(2):
                idx = 2 * cc + half
                if (g, idx - 2) in vevac:
                    t_eng.wait_ge(sms, vevac[(g, idx - 2)])
                elif g >= 1 and idx < 2:
                    t_eng.wait_ge(sms, vevac[(g - 1, 2 + idx)])
                for hcc in range(HC):
                    t_eng.matmul(psVs[idx % 2][:],
                                 xnTB[:, hcc * 256 + cc * 128: hcc * 256 + (cc + 1) * 128],
                                 WvT[:, hcc * D + half * 512: hcc * D + (half + 1) * 512],
                                 start=(hcc == 0), stop=False)
                ins = t_eng.matmul(psVs[idx % 2][:], ones_sb[0:1, :],
                                   bv_sb[0:1, half * 512:(half + 1) * 512],
                                   start=False, stop=True)
                tk = tinc(ins)
                s_eng.wait_ge(smt, tk)
                ev = s_eng.copy(V_sb[:, tci * D + half * 512: tci * D + (half + 1) * 512],
                                psVs[idx % 2][:])
                vevac[(g, idx)] = sinc(ev)
        v_mm_done[g] = C["t"]
        if (g + 1) in VQ_BOUNDS:
            qi = VQ_BOUNDS.index(g + 1)
            lo, hi = (VQ_BOUNDS[qi - 1] if qi else 0), g + 1
            HWq = (hi - lo) * 2 * D
            o0 = lo * 2 * D
            q_eng.wait_ge(sms, vevac[(g, 3)])
            q_eng.dma_start(out=vx_in[qi][0],
                            in_=V_sb[:, o0:o0 + HWq]).then_inc(stxv[qi], 16)
            g_eng.wait_ge(stxv[qi], 16)
            g_eng.collective_compute("AllGather", ALU.bypass,
                                     replica_groups=PAIR_GROUPS,
                                     ins=[vx_in[qi][:]],
                                     outs=[vx_out[qi][:]]).then_inc(ccs, 1)
            CC["ccs"] += 1
            g_eng.wait_ge(ccs, CC["ccs"])
            for r in range(2):
                dst0 = (r * (QN // 128) * D) + o0
                VL_DONE.append(gdma_start(V_sb[:, dst0:dst0 + HWq],
                                          vx_out[qi][r]))

    qkv_done_ck = C["s"]
    s_xgb.close()
    s_swb.close()

    s_at = ExitStack()
    theta = s_at.enter_context(nc.sbuf_tensor("theta", [128, SEQ], F32))
    P_sb = s_at.enter_context(nc.sbuf_tensor("P_sb", [128, SEQ], BF16))
    PT_sb = s_at.enter_context(nc.sbuf_tensor("PT_sb", [128, SEQ], BF16))
    t1a = s_at.enter_context(nc.sbuf_tensor("t1a", [128, 512], BF16))
    t1b = s_at.enter_context(nc.sbuf_tensor("t1b", [128, 512], BF16))
    t2a = s_at.enter_context(nc.sbuf_tensor("t2a", [128, 512], BF16))
    t2b = s_at.enter_context(nc.sbuf_tensor("t2b", [128, 512], BF16))
    OnA = s_at.enter_context(nc.sbuf_tensor("OnA", [128, D], BF16))
    OnB = s_at.enter_context(nc.sbuf_tensor("OnB", [128, D], BF16))
    psS4 = [s_at.enter_context(nc.psum_tensor(f"psS{i}", [128, 512], F32)) for i in range(4)]
    psO2 = [s_at.enter_context(nc.psum_tensor(f"psO{i}", [128, 1024], F32)) for i in range(2)]

    t1s, t2s = [t1a, t1b], [t2a, t2b]
    psSs = psS4
    Ons = [OnA, OnB]
    stOns = [stOn0, stOn1]
    sok = ["so0", "so1"]

    lut_ck = {}
    thw_ck = {}
    score_ck = {}
    pt_ck = {}
    ptev_ck = {}
    av_ck = {}
    oev_ck = {}
    exp_ck = {}
    negm_ck = {}
    recip_ck = {}
    onst_at = {}

    for i in range(TILES + 1):
        j = i - 1
        if j >= 0:
            q_eng.wait_ge(sms, exp_ck[j])
            if j == 0:
                for _sem, _val in dedup_waits(KL_DONE + VL_DONE
                                              + w1_pre + w2_pre):
                    q_eng.wait_ge(_sem, _val)
            if j >= 1:
                q_eng.wait_ge(smt, av_ck[j - 1])
            q_eng.dma_start(out=PT_sb.rearrange("p (tc j2) -> p tc j2", tc=TC),
                            in_=P_sb[:, :], transpose=True).then_inc(ldpt, 16)
            C["ldpt"] = C.get("ldpt", 0) + 16
            q_eng.wait_ge(ldpt, C["ldpt"])
        if i < TILES:
            if i == 0:
                t_eng.wait_ge(sms, sweepA_s_ck)
            for w in range(WIN):
                widx = i * WIN + w
                if i == 0 and w == 0:
                    for _sem, _val in dedup_waits(KL_DONE):
                        t_eng.wait_ge(_sem, _val)
                if widx >= 4:
                    pi, pw = divmod(widx - 4, WIN)
                    t_eng.wait_ge(sms, lut_ck[(pi, pw)])
                ins = None
                t0 = w * 512
                kbase = (0 if t0 < QN else HC * QN)
                tl = t0 if t0 < QN else t0 - QN
                for ec in range(HC):
                    ins = t_eng.matmul(
                        psSs[widx % 4][:],
                        Q_sb[:, ec * QN + i * 128: ec * QN + (i + 1) * 128],
                        K_sb[:, kbase + ec * QN + tl: kbase + ec * QN + tl + 512],
                        start=(ec == 0), stop=(ec == HC - 1))
                score_ck[(i, w)] = tinc(ins)
                s_eng.wait_ge(smt, score_ck[(i, w)])
                if widx >= 2:
                    pi, pw = divmod(widx - 2, WIN)
                    s_eng.wait_ge(smv, thw_ck[(pi, pw)])
                s_eng.activation(t1s[widx % 2][:], psSs[widx % 4][:], AF.Tanh, scale=0.5)
                s_eng.activation(t2s[widx % 2][:], psSs[widx % 4][:], AF.Tanh)
                rl = s_eng.activation(theta[:, w * 512:(w + 1) * 512],
                                      psSs[widx % 4][:], AF.Relu, scale=0.1)
                lut_ck[(i, w)] = sinc(rl)
                v_eng.wait_ge(sms, lut_ck[(i, w)])
                if i >= 1 and w == 0:
                    v_eng.wait_ge(sms, exp_ck[i - 1])
                v_eng.scalar_tensor_tensor(t1s[widx % 2][:], t1s[widx % 2][:],
                                           2.0 / 3.0, t2s[widx % 2][:],
                                           ALU.mult, ALU.add)
                v_eng.drain()
                st2 = v_eng.scalar_tensor_tensor(theta[:, w * 512:(w + 1) * 512],
                                                 t1s[widx % 2][:], 0.15,
                                                 theta[:, w * 512:(w + 1) * 512],
                                                 ALU.mult, ALU.add)
                v_eng.drain()
                mx = v_eng.reduce_max(mcol[:, w:w + 1], theta[:, w * 512:(w + 1) * 512],
                                      axis=mybir.AxisListType.X)
                thw_ck[(i, w)] = vinc(mx)

        if j >= 0:
            t_eng.wait_ge(ldpt, 16 * (j + 1))
            if j >= 2:
                t_eng.wait_ge(sms, oev_ck[j - 2])
            ins = None
            if j == 0 and NVQ >= 2:
                bnd = 2 * VQ_BOUNDS[NVQ - 2]
                part1 = list(range(0, bnd)) + list(range(TC // 2, TC // 2 + bnd))
                part2 = [tci for tci in range(TC) if tci not in part1]
                for _sem, _val in dedup_waits(VL_DONE[:2 * (NVQ - 1)]):
                    t_eng.wait_ge(_sem, _val)
                for half in range(2):
                    for tci in part1:
                        t_eng.matmul(
                            psO2[0][:, half * 512:(half + 1) * 512],
                            PT_sb[:, tci * 128:(tci + 1) * 128],
                            V_sb[:, tci * D + half * 512: tci * D + (half + 1) * 512],
                            start=(tci == part1[0]), stop=False)
                for _sem, _val in dedup_waits(VL_DONE):
                    t_eng.wait_ge(_sem, _val)
                for half in range(2):
                    for tci in part2:
                        ins = t_eng.matmul(
                            psO2[0][:, half * 512:(half + 1) * 512],
                            PT_sb[:, tci * 128:(tci + 1) * 128],
                            V_sb[:, tci * D + half * 512: tci * D + (half + 1) * 512],
                            start=False, stop=(tci == part2[-1]))
            else:
                if j == 0:
                    for _sem, _val in dedup_waits(VL_DONE):
                        t_eng.wait_ge(_sem, _val)
                for half in range(2):
                    for tci in range(TC):
                        ins = t_eng.matmul(
                            psO2[j % 2][:, half * 512:(half + 1) * 512],
                            PT_sb[:, tci * 128:(tci + 1) * 128],
                            V_sb[:, tci * D + half * 512: tci * D + (half + 1) * 512],
                            start=(tci == 0), stop=(tci == TC - 1))
            av_ck[j] = tinc(ins)

        if i < TILES:
            v_eng.drain()
            v_eng.reduce_max(mrow[:], mcol[:, 0:WIN], axis=mybir.AxisListType.X)
            v_eng.drain()
            ng = v_eng.tensor_scalar(negm[:], mrow[:], -1.0, None, ALU.mult)
            negm_ck[i] = vinc(ng)
        if j >= 0:
            v_eng.wait_ge(sms, exp_ck[j])
            if j >= 1:
                v_eng.wait_ge(sms, oev_ck[j - 1])
            rc = v_eng.reciprocal(rrow[:], lrow[:])
            recip_ck[j] = vinc(rc)

        if i < TILES:
            if i >= 1:
                s_eng.wait_ge(smv, recip_ck[i - 1])
                s_eng.wait_ge(ldpt, 16 * i)
            else:
                s_eng.wait_ge(smv, negm_ck[0])
            ex = s_eng.activation(P_sb[:], theta[:], AF.Exp,
                                  bias=negm[:], scale=1.0, accum_out=lrow[:])
            exp_ck[i] = sinc(ex)
            s_eng.drain()
        if j >= 0:
            s_eng.wait_ge(smt, av_ck[j])
            s_eng.wait_ge(smv, recip_ck[j])
            if j >= 2:
                s_eng.wait_ge(stOns[j % 2], onst_at[j - 2])
            ev = s_eng.activation(Ons[j % 2][:], psO2[j % 2][:], AF.Identity,
                                  bias=0.0, scale=rrow[:])
            oev_ck[j] = sinc(ev)

        if j >= 0:
            q_eng.wait_ge(sms, oev_ck[j])
            q_eng.dma_start(out=on_scr[j * 128:(j + 1) * 128, :],
                            in_=Ons[j % 2][:]).then_inc(stOns[j % 2], 16)
            C[sok[j % 2]] += 16
            onst_at[j] = C[sok[j % 2]]
            q_eng.wait_ge(stOns[j % 2], onst_at[j])

    attn_end_s = C["s"]
    ST_ON_TOT = (C["so0"], C["so1"])
    s_at.close()
    s_v.close()
    s_kq.close()

    s_f = ExitStack()
    W1_sb = s_f.enter_context(nc.sbuf_tensor("W1_sb", [128, HC * INTER], BF16))
    W2_sb = s_f.enter_context(nc.sbuf_tensor("W2_sb", [128, IC * D], BF16))
    WoT = s_f.enter_context(nc.sbuf_tensor("WoT", [128, HC * D], BF16))
    H_sb = s_f.enter_context(nc.sbuf_tensor("H_sb", [128, IC * 256], BF16))
    x2nT2 = [s_f.enter_context(nc.sbuf_tensor(f"x2nT{i}", [128, HC * 256], BF16)) for i in range(2)]
    x2_2 = [s_f.enter_context(nc.sbuf_tensor(f"x2_{i}", [128, 2 * D], F32)) for i in range(2)]
    xr_sb = s_f.enter_context(nc.sbuf_tensor("xr_sb", [128, D], F32))
    On_sb = s_f.enter_context(nc.sbuf_tensor("On_sb", [128, D], BF16))
    y_sb = s_f.enter_context(nc.sbuf_tensor("y_sb", [128, D], F32))
    OT2 = [s_f.enter_context(nc.sbuf_tensor(f"OT{i}", [128, HC * 128], BF16)) for i in range(2)]
    xnF = s_f.enter_context(nc.sbuf_tensor("xnF", [128, D], BF16))
    psTf0 = s_f.enter_context(nc.psum_tensor("psTf0", [128, 1024], BF16))
    psTf1 = s_f.enter_context(nc.psum_tensor("psTf1", [128, 1024], BF16))
    psY = s_f.enter_context(nc.psum_tensor("psY", [128, 1024], F32))
    psH0 = s_f.enter_context(nc.psum_tensor("psH0", [128, 512], F32))
    psH1 = s_f.enter_context(nc.psum_tensor("psH1", [128, 512], F32))
    psY2 = s_f.enter_context(nc.psum_tensor("psY2", [128, 1024], F32))

    g_eng.wait_ge(sms, attn_end_s)
    g_eng.wait_ge(stOn0, ST_ON_TOT[0])
    g_eng.wait_ge(stOn1, ST_ON_TOT[1])
    go = gdma_start(H_sb[:, 0:HC * D].rearrange("p (e d) -> p e d", e=HC),
                    Wo.rearrange("(e p) d -> p e d", p=128))
    for _s, _v in dedup_waits(w1_pre):
        g_eng.wait_ge(_s, _v)
    w1v = w1bf.rearrange("(q h p) i -> q p h i", q=2, p=128)
    w1_gd = []
    for qq in range(2):
        w1_gd.append(gdma_start(
            W1_sb[:, qq * 4 * INTER:(qq + 1) * 4 * INTER].rearrange("p (h i) -> p h i", h=4),
            w1v[qq]))
    for _s, _v in dedup_waits(w2_pre):
        g_eng.wait_ge(_s, _v)
    w2v = w2bf.rearrange("(q ic p) h -> q p ic h", q=4, p=128)
    w2_gd = []
    for qq in range(4):
        w2_gd.append(gdma_start(
            W2_sb[:, qq * 8 * D:(qq + 1) * 8 * D].rearrange("p (ic h) -> p ic h", ic=8),
            w2v[qq]))
    wo_ck = emit_w_transpose(H_sb, WoT, go, [psTf0, psTf1])

    onv1 = on_scr.rearrange("(k p) d -> k p d", p=128)
    xrv1 = xb.rearrange("(k p) d -> k p d", p=128)

    ontr_ck = {}
    xntr_ck = {}
    otev_ck = {}
    yp_ck = {}
    x2_ck = {}
    x2nt_ck = {}
    h_ck = {}
    hev_ck = {}
    f2_ck = {}
    yv_ck = {}
    lon_at = {}
    lxr_at = {}
    apply_ck2 = {}
    psTfs = [psTf0, psTf1]
    psHs = [psH0, psH1]

    for u in range(FS + 1):
        if u < FS:
            xp = u % 2
            for sg in range(2):
                if (u, sg) == (0, 0):
                    q_eng.wait_ge(stOn0, ST_ON_TOT[0])
                    q_eng.wait_ge(stOn1, ST_ON_TOT[1])
                else:
                    psg = (u, sg - 1) if sg == 1 else (u - 1, 1)
                    q_eng.wait_ge(smt, ontr_ck[psg])
                q_eng.dma_start(out=On_sb[:], in_=onv1[u * 2 + sg]).then_inc(lOn, 16)
                C["lon"] += 16
                lon_at[(u, sg)] = C["lon"]
                t_eng.wait_ge(lOn, lon_at[(u, sg)])
                if u == 0 and sg == 0:
                    t_eng.wait_ge(sms, wo_ck)
                if u >= 1:
                    t_eng.wait_ge(sms, x2nt_ck[(u - 1, sg)])
                ins = None
                pst = psTfs[sg % 2]
                for hcc in range(HC):
                    ins = t_eng.transpose(pst[:, hcc * 128:(hcc + 1) * 128],
                                          On_sb[:, hcc * 128:(hcc + 1) * 128],
                                          eyesb[:])
                ontr_ck[(u, sg)] = tinc(ins)
                s_eng.wait_ge(smt, ontr_ck[(u, sg)])
                if u >= 1:
                    s_eng.wait_ge(smt, yp_ck[(u - 1, sg)])
                cp = s_eng.copy(OT2[sg][:], pst[:])
                otev_ck[(u, sg)] = sinc(cp)

            for sg in range(2):
                if (u, sg) != (0, 0):
                    psg = (u, sg - 1) if sg == 1 else (u - 1, 1)
                    q_eng.wait_ge(smv, x2_ck[psg])
                q_eng.dma_start(out=xr_sb[:], in_=xrv1[u * 2 + sg]).then_inc(lxr, 16)
                C["lxr"] += 16
                lxr_at[(u, sg)] = C["lxr"]
                t_eng.wait_ge(sms, otev_ck[(u, sg)])
                if (u, sg) != (0, 0):
                    psg = (u, sg - 1) if sg == 1 else (u - 1, 1)
                    t_eng.wait_ge(smv, x2_ck[psg])
                ins = None
                for half in range(2):
                    for hcc in range(HC):
                        t_eng.matmul(psY[:, half * 512:(half + 1) * 512],
                                     OT2[sg][:, hcc * 128:(hcc + 1) * 128],
                                     WoT[:, hcc * D + half * 512: hcc * D + (half + 1) * 512],
                                     start=(hcc == 0), stop=False)
                    ins = t_eng.matmul(psY[:, half * 512:(half + 1) * 512],
                                       ones_sb[0:1, :], bo_sb[0:1, half * 512:(half + 1) * 512],
                                       start=False, stop=True)
                yp_ck[(u, sg)] = tinc(ins)

                v_eng.wait_ge(smt, yp_ck[(u, sg)])
                v_eng.wait_ge(lxr, lxr_at[(u, sg)])
                if u >= 2 and sg == 0:
                    v_eng.wait_ge(sms, x2nt_ck[(u - 2, 1)])
                ad = v_eng.tensor_tensor(x2_2[xp][:, sg * D:(sg + 1) * D], psY[:],
                                         xr_sb[:], ALU.add)
                x2_ck[(u, sg)] = vinc(ad)

                if sg == 0:
                    apl = ln_chunk(x2_2[xp], 0, xnF,
                                   xntr_ck.get((u - 1, 1)),
                                   smv, x2_ck[(u, 0)])
                    apply_ck2[(u, 0)] = apl

        def emit_xntr(uu, sg, extra_scalar_wait=None):
            t_eng.wait_ge(sms, apply_ck2[(uu, sg)])
            ins = None
            pst = psTfs[sg % 2]
            for hcc in range(HC):
                ins = t_eng.transpose(pst[:, hcc * 128:(hcc + 1) * 128],
                                      xnF[:, hcc * 128:(hcc + 1) * 128], eyesb[:])
            tk = tinc(ins)
            xntr_ck[(uu, sg)] = tk
            if sg == 0:
                apl = ln_chunk(x2_2[uu % 2], D, xnF, tk, smv, x2_ck[(uu, 1)])
                apply_ck2[(uu, 1)] = apl
            s_eng.wait_ge(smt, tk)
            if extra_scalar_wait:
                s_eng.wait_ge(smt, extra_scalar_wait)
            cp = s_eng.copy(
                x2nT2[uu % 2].rearrange("p (h s) -> p h s", h=HC)[:, :, sg * 128:(sg + 1) * 128],
                pst.rearrange("p (h j) -> p h j", h=HC))
            x2nt_ck[(uu, sg)] = sinc(cp)

        if u == 0:
            emit_xntr(0, 0)
            emit_xntr(0, 1)

        if u >= 1:
            v = u - 1
            vp = v % 2
            t_eng.wait_ge(sms, x2nt_ck[(v, 1)])
            NP = IC // 2
            for icp in range(NP):
                if v == 0 and icp == 0:
                    t_eng.wait_ge(w1_gd[0][0], w1_gd[0][1])
                    t_eng.wait_ge(w1_gd[1][0], w1_gd[1][1])
                if icp >= 2:
                    t_eng.wait_ge(sms, hev_ck[(v, icp - 2)])
                elif v >= 1:
                    t_eng.wait_ge(sms, hev_ck[(v - 1, NP - 2 + icp)])
                ins = None
                for sub in range(2):
                    ic = 2 * icp + sub
                    for hcc in range(HC):
                        ins = t_eng.matmul(psHs[icp % 2][:, sub * 256:(sub + 1) * 256],
                                           W1_sb[:, hcc * INTER + ic * 128: hcc * INTER + (ic + 1) * 128],
                                           x2nT2[vp][:, hcc * 256:(hcc + 1) * 256],
                                           start=(hcc == 0), stop=(hcc == HC - 1))
                h_ck[(v, icp)] = tinc(ins)
                s_eng.wait_ge(smt, h_ck[(v, icp)])
                if v >= 1 and icp == 0:
                    s_eng.wait_ge(smt, f2_ck[(v - 1, 1)])
                ev = None
                for sub in range(2):
                    ic = 2 * icp + sub
                    ev = s_eng.activation(H_sb[:, ic * 256:(ic + 1) * 256],
                                          psHs[icp % 2][:, sub * 256:(sub + 1) * 256], AF.Relu,
                                          bias=b1_sb[:, ic:ic + 1], scale=1.0)
                hev_ck[(v, icp)] = sinc(ev)

            if u < FS:
                emit_xntr(u, 0, extra_scalar_wait=h_ck[(u - 1, IC // 2 - 1)])
                emit_xntr(u, 1)

            for sg in range(2):
                t_eng.wait_ge(sms, hev_ck[(v, IC // 2 - 1)])
                if (v, sg) != (0, 0):
                    psg = (v, sg - 1) if sg == 1 else (v - 1, 1)
                    t_eng.wait_ge(smv, yv_ck[psg])
                ins = None
                for half in range(2):
                    for ic in range(IC):
                        if v == 0 and sg == 0 and half == 0 and ic % 8 == 0:
                            t_eng.wait_ge(w2_gd[ic // 8][0], w2_gd[ic // 8][1])
                        t_eng.matmul(psY2[:, half * 512:(half + 1) * 512],
                                     H_sb[:, ic * 256 + sg * 128: ic * 256 + (sg + 1) * 128],
                                     W2_sb[:, ic * D + half * 512: ic * D + (half + 1) * 512],
                                     start=(ic == 0), stop=False)
                    ins = t_eng.matmul(psY2[:, half * 512:(half + 1) * 512],
                                       ones_sb[0:1, :], b2_sb[0:1, half * 512:(half + 1) * 512],
                                       start=False, stop=True)
                f2_ck[(v, sg)] = tinc(ins)
                v_eng.wait_ge(smt, f2_ck[(v, sg)])
                sidx = 2 * v + sg
                if sidx >= 1:
                    v_eng.wait_ge(sty, 16 * sidx)
                ad = v_eng.tensor_tensor(y_sb[:], psY2[:],
                                         x2_2[vp][:, sg * D:(sg + 1) * D], ALU.add)
                yv_ck[(v, sg)] = vinc(ad)
                s_eng.wait_ge(smv, yv_ck[(v, sg)])
                s_eng.dma_start(out=y[sidx * 128:(sidx + 1) * 128, :],
                                in_=y_sb[:]).then_inc(sty, 16)
                C["sty"] += 16

    q_eng.wait_ge(sty, C["sty"])

    s_f.close()
    top.close()
    return nc


_NC_CACHE = {}


def get_nc(SEQ=4096):
    if SEQ not in _NC_CACHE:
        _NC_CACHE[SEQ] = build_nc(SEQ)
    return _NC_CACHE[SEQ]


def core_inputs(x_full, weights, SEQ=4096):
    B = x_full.shape[0]
    eye_np = np.eye(128, dtype=BF)
    ones_np = np.ones((1, 128), dtype=BF)
    QN = SEQ // 2
    in_maps = []
    for c in range(2 * B):
        b, h = divmod(c, 2)
        xbat = x_full[b]
        xbp = np.ascontiguousarray(
            np.concatenate([xbat[h * QN:(h + 1) * QN], xbat[(1 - h) * QN:(2 - h) * QN]], axis=0))
        m = dict(weights)
        m["xb"] = xbp
        m["eye"] = eye_np
        m["ones"] = ones_np
        in_maps.append(m)
    return in_maps


def kernel(L0_attn_x, L0_attn_Wq, L0_attn_bq, L0_attn_Wk, L0_attn_bk,
           L0_attn_Wv, L0_attn_bv, L0_attn_Wo, L0_attn_bo,
           L0_ffn_W1, L0_ffn_b1, L0_ffn_W2, L0_ffn_b2):
    SEQ = 4096
    QN = SEQ // 2
    x = np.ascontiguousarray(np.asarray(L0_attn_x, dtype=np.float32))
    weights = dict(
        Wq=np.ascontiguousarray(np.asarray(L0_attn_Wq, dtype=np.float32)),
        Wk=np.ascontiguousarray(np.asarray(L0_attn_Wk, dtype=np.float32)),
        Wv=np.ascontiguousarray(np.asarray(L0_attn_Wv, dtype=np.float32)),
        Wo=np.ascontiguousarray(np.asarray(L0_attn_Wo, dtype=np.float32)),
        bq=np.ascontiguousarray(np.asarray(L0_attn_bq, dtype=np.float32)),
        bk=np.ascontiguousarray(np.asarray(L0_attn_bk, dtype=np.float32)),
        bv=np.ascontiguousarray(np.asarray(L0_attn_bv, dtype=np.float32)),
        bo=np.ascontiguousarray(np.asarray(L0_attn_bo, dtype=np.float32)),
        W1=np.ascontiguousarray(np.asarray(L0_ffn_W1, dtype=np.float32)),
        b1=np.ascontiguousarray(np.asarray(L0_ffn_b1, dtype=np.float32)),
        W2=np.ascontiguousarray(np.asarray(L0_ffn_W2, dtype=np.float32)),
        b2=np.ascontiguousarray(np.asarray(L0_ffn_b2, dtype=np.float32)),
    )
    nc = get_nc(SEQ)
    in_maps = core_inputs(x, weights, SEQ)
    res = run_bass_kernel_spmd(nc, in_maps, core_ids=list(range(8)))
    out = np.empty((4, SEQ, 1024), dtype=np.float32)
    for c in range(8):
        b, h = divmod(c, 2)
        out[b, h * QN:(h + 1) * QN] = res.results[c]["y"]
    return out


# version 31
# speedup vs baseline: 1.0969x; 1.0969x over previous
import numpy as np
import ml_dtypes
from contextlib import ExitStack

import concourse.bass as bass
import concourse.mybir as mybir
from concourse.bass_utils import run_bass_kernel_spmd
from concourse import library_config

F32 = mybir.dt.float32
BF16 = mybir.dt.bfloat16
AF = mybir.ActivationFunctionType
ALU = mybir.AluOpType
BF = ml_dtypes.bfloat16

D = 1024
HC = 8
INTER = 4096
IC = 32
EPS = 1e-5


def build_nc(SEQ=4096):
    QN = SEQ // 2
    G = SEQ // 256
    GQ = G // 2
    TILES = QN // 128
    WIN = SEQ // 512
    TC = SEQ // 128
    PB = TC // 8
    FS = QN // 256

    nc = bass.Bass("TRN2", target_bir_lowering=False)
    xb = nc.dram_tensor("xb", [SEQ, D], F32, kind="ExternalInput")
    Wq = nc.dram_tensor("Wq", [D, D], F32, kind="ExternalInput")
    Wk = nc.dram_tensor("Wk", [D, D], F32, kind="ExternalInput")
    Wv = nc.dram_tensor("Wv", [D, D], F32, kind="ExternalInput")
    Wo = nc.dram_tensor("Wo", [D, D], F32, kind="ExternalInput")
    bq = nc.dram_tensor("bq", [D], F32, kind="ExternalInput")
    bk = nc.dram_tensor("bk", [D], F32, kind="ExternalInput")
    bv = nc.dram_tensor("bv", [D], F32, kind="ExternalInput")
    bo = nc.dram_tensor("bo", [D], F32, kind="ExternalInput")
    W1 = nc.dram_tensor("W1", [D, INTER], F32, kind="ExternalInput")
    b1 = nc.dram_tensor("b1", [INTER], F32, kind="ExternalInput")
    W2 = nc.dram_tensor("W2", [INTER, D], F32, kind="ExternalInput")
    b2 = nc.dram_tensor("b2", [D], F32, kind="ExternalInput")
    eye = nc.dram_tensor("eye", [128, 128], BF16, kind="ExternalInput")
    ones = nc.dram_tensor("ones", [1, 128], BF16, kind="ExternalInput")
    y = nc.dram_tensor("y", [QN, D], F32, kind="ExternalOutput")
    wobf = nc.dram_tensor("wobf", [D, D], BF16, kind="Internal")
    on_scr = nc.dram_tensor("on_scr", [QN, D], BF16, kind="Internal")
    NVQ0 = min(4, SEQ // 512)
    KQ_B0 = [((SEQ // 512) * (q + 1)) // NVQ0 for q in range(NVQ0)]
    kxq_in = [nc.dram_tensor(f"kxq_in{i}", [1, 128, HC * (KQ_B0[i] - (KQ_B0[i - 1] if i else 0)) * 256], BF16, kind="Internal") for i in range(NVQ0)]
    kxq_out = [nc.dram_tensor(f"kxq_out{i}", [2, 128, HC * (KQ_B0[i] - (KQ_B0[i - 1] if i else 0)) * 256], BF16, kind="Internal") for i in range(NVQ0)]
    QCOLS = ((SEQ // 512) // NVQ0) * 2 * D
    vx_in = [nc.dram_tensor(f"vx_in{i}", [1, 128, QCOLS], BF16, kind="Internal") for i in range(NVQ0)]
    vx_out = [nc.dram_tensor(f"vx_out{i}", [2, 128, QCOLS], BF16, kind="Internal") for i in range(NVQ0)]

    t_eng = nc.tensor
    s_eng = nc.scalar
    v_eng = nc.vector
    g_eng = nc.gpsimd
    q_eng = nc.sync

    top = ExitStack()
    E = top.enter_context

    gdma = E(nc.semaphore("gdma"))
    gdmas = [gdma] + [E(nc.semaphore(f"gdma{i}")) for i in range(1, 4)]
    ldc = E(nc.semaphore("ldc"))
    ldx0 = E(nc.semaphore("ldx0"))
    ldx1 = E(nc.semaphore("ldx1"))
    ldx2 = E(nc.semaphore("ldx2"))
    smt = E(nc.semaphore("smt"))
    sms = E(nc.semaphore("sms"))
    smv = E(nc.semaphore("smv"))
    stOn0 = E(nc.semaphore("stOn0"))
    stOn1 = E(nc.semaphore("stOn1"))
    lOn = E(nc.semaphore("lOn"))
    lxr = E(nc.semaphore("lxr"))
    sty = E(nc.semaphore("sty"))
    stw = E(nc.semaphore("stw"))
    ldwo = E(nc.semaphore("ldwo"))
    ldw1 = E(nc.semaphore("ldw1"))
    ldw2 = E(nc.semaphore("ldw2"))
    ldpt = E(nc.semaphore("ldpt"))
    ccs = E(nc.semaphore("ccs"))
    stxk = [E(nc.semaphore(f"stxk{i}")) for i in range(4)]
    stxv = [E(nc.semaphore(f"stxv{i}")) for i in range(4)]

    C = dict(gd=0, lc=0, lx0=0, lx1=0, lx2=0, t=0, s=0, v=0, so0=0, so1=0,
             lon=0, lxr=0, sty=0)

    def tinc(ins):
        ins.then_inc(smt, 1); C["t"] += 1; return C["t"]

    def sinc(ins):
        ins.then_inc(sms, 1); C["s"] += 1; return C["s"]

    def vinc(ins):
        ins.then_inc(smv, 1); C["v"] += 1; return C["v"]

    GDC = [0, 0, 0, 0]
    GDN = [0]

    def gdma_start(out, in_):
        i = GDN[0] % 4
        GDN[0] += 1
        if GDC[i] > 0:
            g_eng.wait_ge(gdmas[i], GDC[i])
        g_eng.dma_start(out=out, in_=in_).then_inc(gdmas[i], 16)
        GDC[i] += 16
        return (gdmas[i], GDC[i])

    eyesb = E(nc.sbuf_tensor("eyesb", [128, 128], BF16))
    ones_sb = E(nc.sbuf_tensor("ones_sb", [1, 128], BF16))
    bq_sb = E(nc.sbuf_tensor("bq_sb", [128, HC], F32))
    bk_sb = E(nc.sbuf_tensor("bk_sb", [128, HC], F32))
    b1_sb = E(nc.sbuf_tensor("b1_sb", [128, IC], F32))
    bv_sb = E(nc.sbuf_tensor("bv_sb", [1, D], BF16))
    bo_sb = E(nc.sbuf_tensor("bo_sb", [1, D], BF16))
    b2_sb = E(nc.sbuf_tensor("b2_sb", [1, D], BF16))
    stats2 = [E(nc.sbuf_tensor(f"stats{i}", [128, 12], F32)) for i in range(2)]
    mvt2 = [E(nc.sbuf_tensor(f"mvt{i}", [128, 2], F32)) for i in range(2)]
    rstd2 = [E(nc.sbuf_tensor(f"rstd{i}", [128, 1], F32)) for i in range(2)]
    nmr2 = [E(nc.sbuf_tensor(f"nmr{i}", [128, 1], F32)) for i in range(2)]
    mrow = E(nc.sbuf_tensor("mrow", [128, 1], F32))
    negm = E(nc.sbuf_tensor("negm", [128, 1], F32))
    lrow = E(nc.sbuf_tensor("lrow", [128, 1], F32))
    rrow = E(nc.sbuf_tensor("rrow", [128, 1], F32))
    mcol = E(nc.sbuf_tensor("mcol", [128, 16], F32))

    q_eng.dma_start(out=eyesb[:], in_=eye[:, :]).then_inc(ldc, 16); C["lc"] += 16
    q_eng.dma_start(out=ones_sb[:], in_=ones[:, :]).then_inc(ldc, 16); C["lc"] += 16
    q_eng.dma_start(out=bq_sb[:], in_=bq.rearrange("(e p) -> p e", p=128), allow_slow_non_contiguous=True).then_inc(ldc, 16); C["lc"] += 16
    q_eng.dma_start(out=bk_sb[:], in_=bk.rearrange("(e p) -> p e", p=128), allow_slow_non_contiguous=True).then_inc(ldc, 16); C["lc"] += 16
    q_eng.dma_start(out=b1_sb[:], in_=b1.rearrange("(i p) -> p i", p=128), allow_slow_non_contiguous=True).then_inc(ldc, 16); C["lc"] += 16
    LC_TOTAL = C["lc"]

    GD_BIASES = [gdma_start(bv_sb[:], bv[None, :]),
                 gdma_start(bo_sb[:], bo[None, :]),
                 gdma_start(b2_sb[:], b2[None, :])]

    def emit_w_transpose(wtmp_sb, WT_sb, gd_done, pss, base=0, prev_ck=0):
        evac_ck = {}
        nb = len(pss)
        for ec in range(HC):
            if ec == 0:
                t_eng.wait_ge(gd_done[0], gd_done[1])
                t_eng.wait_ge(ldc, LC_TOTAL)
                if prev_ck:
                    t_eng.wait_ge(sms, prev_ck)
            if ec >= nb:
                t_eng.wait_ge(sms, evac_ck[ec - nb])
            ins = None
            for hcc in range(HC):
                ins = t_eng.transpose(pss[ec % nb][:, hcc * 128:(hcc + 1) * 128],
                                      wtmp_sb[:, base + ec * D + hcc * 128: base + ec * D + (hcc + 1) * 128],
                                      eyesb[:])
            tk = tinc(ins)
            s_eng.wait_ge(smt, tk)
            cp = s_eng.copy(WT_sb.rearrange("p (h e) -> p h e", h=HC)[:, :, ec * 128:(ec + 1) * 128],
                            pss[ec % nb].rearrange("p (h j) -> p h j", h=HC))
            evac_ck[ec] = sinc(cp)
        return evac_ck[HC - 1]

    def emit_ln_chunk(xg_sb, col0, xn_sb, prev2_apply_ck, xn_free_t_ck,
                      ld_sem=None, ld_val=0, par=0):
        stats, mvt, rstd, nmr = stats2[par], mvt2[par], rstd2[par], nmr2[par]
        if ld_sem is not None:
            v_eng.wait_ge(ld_sem, ld_val)
        if prev2_apply_ck:
            v_eng.wait_ge(sms, prev2_apply_ck)
        v_eng.bn_stats(stats[:, 0:6], xg_sb[:, col0:col0 + 512])
        v_eng.bn_stats(stats[:, 6:12], xg_sb[:, col0 + 512:col0 + D])
        v_eng.drain()
        v_eng.bn_aggr(mvt[:], stats[:])
        v_eng.drain()
        v_eng.tensor_scalar(rstd[:], mvt[:, 1:2], EPS, None, ALU.add)
        v_eng.drain()
        v_eng.reciprocal(rstd[:], rstd[:])
        ts = v_eng.tensor_scalar(nmr[:], mvt[:, 0:1], -1.0, None, ALU.mult)
        v1 = vinc(ts)
        s_eng.wait_ge(smv, v1)
        s_eng.activation(rstd[:], rstd[:], AF.Sqrt)
        s_eng.drain()
        s_eng.activation(nmr[:], nmr[:], AF.Identity, scale=rstd[:])
        s_eng.drain()
        if xn_free_t_ck:
            s_eng.wait_ge(smt, xn_free_t_ck)
        ap = s_eng.activation(xn_sb[:], xg_sb[:, col0:col0 + D], AF.Identity,
                              bias=nmr[:], scale=rstd[:])
        return sinc(ap)

    PAIR_GROUPS = [[0, 1], [2, 3], [4, 5], [6, 7]]
    CC = dict(ccs=0)
    KL_DONE = []
    VL_DONE = []

    def dedup_waits(tups):
        m = {}
        for sem, val in tups:
            if id(sem) not in m or m[id(sem)][1] < val:
                m[id(sem)] = (sem, val)
        return list(m.values())

    ln_last = {0: 0, 1: 0}
    ln_k = [0]

    def ln_chunk(xg_sb, col0, xn_sb, xn_free_t_ck, ld_sem=None, ld_val=0):
        par = ln_k[0] % 2
        ck = emit_ln_chunk(xg_sb, col0, xn_sb, ln_last[par], xn_free_t_ck,
                           ld_sem, ld_val, par=par)
        ln_last[par] = ck
        ln_k[0] += 1
        return ck

    s_kq = ExitStack()
    K_sb = s_kq.enter_context(nc.sbuf_tensor("K_sb", [128, HC * SEQ], BF16))
    Q_sb = s_kq.enter_context(nc.sbuf_tensor("Q_sb", [128, HC * QN], BF16))
    if HC * QN >= HC * D:
        wv_stage, wv_base = K_sb, HC * QN
    else:
        wv_stage = s_kq.enter_context(nc.sbuf_tensor("wvstg", [128, HC * D], BF16))
        wv_base = 0

    s_swa = ExitStack()
    WqT = s_swa.enter_context(nc.sbuf_tensor("WqT", [128, HC * D], BF16))
    WkT = s_swa.enter_context(nc.sbuf_tensor("WkT", [128, HC * D], BF16))
    psTa = s_swa.enter_context(nc.psum_tensor("psTa", [128, 1024], BF16))
    psTb = s_swa.enter_context(nc.psum_tensor("psTb", [128, 1024], BF16))
    psTa2 = s_swa.enter_context(nc.psum_tensor("psTa2", [128, 1024], BF16))
    psTb2 = s_swa.enter_context(nc.psum_tensor("psTb2", [128, 1024], BF16))

    s_wtmp = ExitStack()
    wtmpA = s_wtmp.enter_context(nc.sbuf_tensor("wtmpA", [128, HC * D], BF16))
    wtmpA2 = s_wtmp.enter_context(nc.sbuf_tensor("wtmpA2", [128, HC * D], BF16))
    gq = gdma_start(wtmpA.rearrange("p (e d) -> p e d", e=HC),
                    Wq.rearrange("(e p) d -> p e d", p=128))
    g_eng.wait_ge(gq[0], gq[1])
    gk = gdma_start(wtmpA2.rearrange("p (e d) -> p e d", e=HC),
                    Wk.rearrange("(e p) d -> p e d", p=128))
    g_eng.wait_ge(gk[0], gk[1])
    gv = gdma_start(wv_stage[:, wv_base:wv_base + HC * D].rearrange("p (e d) -> p e d", e=HC),
                    Wv.rearrange("(e p) d -> p e d", p=128))
    g_eng.wait_ge(gv[0], gv[1])
    wo_pre = [gdma_start(wobf[r:r + 256, :], Wo[r:r + 256, :])
              for r in range(0, D, 256)]
    wq_ck = emit_w_transpose(wtmpA, WqT, gq, [psTa, psTb, psTa2, psTb2])
    wk_ck = emit_w_transpose(wtmpA2, WkT, gk, [psTa, psTb, psTa2, psTb2],
                             prev_ck=wq_ck)

    s_wtmp.close()

    s_xga = ExitStack()
    xgA0 = s_xga.enter_context(nc.sbuf_tensor("xgA0", [128, 2 * D], F32))
    xgA1 = s_xga.enter_context(nc.sbuf_tensor("xgA1", [128, 2 * D], F32))
    xnA2 = [s_xga.enter_context(nc.sbuf_tensor(f"xnA{i}", [128, D], BF16)) for i in range(2)]
    xnTA2 = [s_xga.enter_context(nc.sbuf_tensor(f"xnTA{i}", [128, HC * 256], BF16)) for i in range(2)]
    psK0 = s_xga.enter_context(nc.psum_tensor("psK0", [128, 512], F32))
    psK1 = s_xga.enter_context(nc.psum_tensor("psK1", [128, 512], F32))
    psQ0 = s_xga.enter_context(nc.psum_tensor("psQ0", [128, 512], F32))
    psQ1 = s_xga.enter_context(nc.psum_tensor("psQ1", [128, 512], F32))

    xgs = [xgA0, xgA1]
    ldxs = [ldx0, ldx1]
    lxk = ["lx0", "lx1"]
    xv = xb.rearrange("(g c p) d -> g p c d", p=128, c=2)

    st_a = dict(xgfree={}, lx_at={})

    def emit_xload(g, state):
        par = g % 2
        if g >= 2:
            q_eng.wait_ge(sms, state["xgfree"][g - 2])
        q_eng.dma_start(out=xgs[par].rearrange("p (c d) -> p c d", c=2),
                        in_=xv[g]).then_inc(ldxs[par], 16)
        C[lxk[par]] += 16
        state["lx_at"][g] = C[lxk[par]]

    q_eng.wait_ge(sms, wq_ck)
    emit_xload(0, st_a)
    if G > 1:
        emit_xload(1, st_a)

    apply_ck = 0
    xnt_evac = {}
    xnt_trans = {}
    kq_mm_done = {}
    kevac = {}
    qevac = {}

    def emit_prep_A(g):
        par = g % 2
        apls = {}
        for cc in range(2):
            k = 2 * g + cc
            apls[cc] = ln_chunk(xgs[par], cc * D, xnA2[k % 2],
                                xnt_trans.get(k - 2), ldxs[par], st_a["lx_at"][g])
            if cc == 1:
                st_a["xgfree"][g] = apls[cc]
                if g + 2 < G // 2:
                    emit_xload(g + 2, st_a)

        for cc in range(2):
            k = 2 * g + cc
            t_eng.wait_ge(sms, apls[cc])
            if k >= 4:
                t_eng.wait_ge(sms, xnt_evac[k - 4])
            ins = None
            pst = [psTa, psTb, psTa2, psTb2][k % 4]
            for hcc in range(HC):
                ins = t_eng.transpose(pst[:, hcc * 128:(hcc + 1) * 128],
                                      xnA2[k % 2][:, hcc * 128:(hcc + 1) * 128], eyesb[:])
            xnt_trans[k] = tinc(ins)
            s_eng.wait_ge(smt, xnt_trans[k])
            if g >= 2 and cc == 0:
                s_eng.wait_ge(smt, kq_mm_done[g - 2])
            cp = s_eng.copy(
                xnTA2[g % 2].rearrange("p (h s) -> p h s", h=HC)[:, :, cc * 128:(cc + 1) * 128],
                pst.rearrange("p (h j) -> p h j", h=HC))
            xnt_evac[k] = sinc(cp)

    G2 = G // 2
    emit_prep_A(0)
    for g in range(G2):
        if g + 1 < G2:
            emit_prep_A(g + 1)
        xnTA = xnTA2[g % 2]
        t_eng.wait_ge(sms, xnt_evac[2 * g + 1])
        if g == 0:
            t_eng.wait_ge(sms, wk_ck)
        psKs = [psK0, psK1]
        for ecp in range(HC // 2):
            if (g, ecp - 2) in kevac:
                t_eng.wait_ge(sms, kevac[(g, ecp - 2)])
            elif g >= 1 and ecp < 2:
                t_eng.wait_ge(sms, kevac[(g - 1, HC // 2 - 2 + ecp)])
            ins = None
            for sub in range(2):
                ec = 2 * ecp + sub
                for hcc in range(HC):
                    ins = t_eng.matmul(psKs[ecp % 2][:, sub * 256:(sub + 1) * 256],
                                       WkT[:, hcc * D + ec * 128: hcc * D + (ec + 1) * 128],
                                       xnTA[:, hcc * 256:(hcc + 1) * 256],
                                       start=(hcc == 0), stop=(hcc == HC - 1))
            tk = tinc(ins)
            s_eng.wait_ge(smt, tk)
            ev = None
            for sub in range(2):
                ec = 2 * ecp + sub
                ev = s_eng.activation(K_sb[:, ec * QN + g * 256: ec * QN + (g + 1) * 256],
                                      psKs[ecp % 2][:, sub * 256:(sub + 1) * 256], AF.Identity,
                                      bias=bk_sb[:, ec:ec + 1], scale=1.0)
            kevac[(g, ecp)] = sinc(ev)
        if True:
            if g == 0:
                t_eng.wait_ge(sms, wq_ck)
            psQs = [psQ0, psQ1]
            for ecp in range(HC // 2):
                if (g, ecp - 2) in qevac:
                    t_eng.wait_ge(sms, qevac[(g, ecp - 2)])
                elif g >= 1 and ecp < 2:
                    t_eng.wait_ge(sms, qevac[(g - 1, HC // 2 - 2 + ecp)])
                ins = None
                for sub in range(2):
                    ec = 2 * ecp + sub
                    for hcc in range(HC):
                        ins = t_eng.matmul(psQs[ecp % 2][:, sub * 256:(sub + 1) * 256],
                                           WqT[:, hcc * D + ec * 128: hcc * D + (ec + 1) * 128],
                                           xnTA[:, hcc * 256:(hcc + 1) * 256],
                                           start=(hcc == 0), stop=(hcc == HC - 1))
                tk = tinc(ins)
                s_eng.wait_ge(smt, tk)
                ev = None
                for sub in range(2):
                    ec = 2 * ecp + sub
                    ev = s_eng.activation(Q_sb[:, ec * QN + g * 256: ec * QN + (g + 1) * 256],
                                          psQs[ecp % 2][:, sub * 256:(sub + 1) * 256], AF.Identity,
                                          bias=bq_sb[:, ec:ec + 1], scale=1.0)
                qevac[(g, ecp)] = sinc(ev)
        kq_mm_done[g] = C["t"]
        if (g + 1) in KQ_B0:
            qi = KQ_B0.index(g + 1)
            lo, hi = (KQ_B0[qi - 1] if qi else 0), g + 1
            q_eng.wait_ge(sms, kevac[(g, HC // 2 - 1)])
            q_eng.dma_start(
                out=kxq_in[qi][0],
                in_=K_sb[:, 0:HC * QN].rearrange("p (e q) -> p e q", e=HC)[:, :, lo * 256:hi * 256],
            ).then_inc(stxk[qi], 16)
            g_eng.wait_ge(stxk[qi], 16)
            g_eng.collective_compute("AllGather", ALU.bypass,
                                     replica_groups=PAIR_GROUPS,
                                     ins=[kxq_in[qi][:]],
                                     outs=[kxq_out[qi][:]]).then_inc(ccs, 1)
            CC["ccs"] += 1

    sweepA_s_ck = C["s"]
    s_xga.close()
    s_swa.close()

    s_v = ExitStack()
    V_sb = s_v.enter_context(nc.sbuf_tensor("V_sb", [128, TC * D], BF16))

    s_swb = ExitStack()
    WvT = s_swb.enter_context(nc.sbuf_tensor("WvT", [128, HC * D], BF16))
    psTc = s_swb.enter_context(nc.psum_tensor("psTc", [128, 1024], BF16))
    psTd = s_swb.enter_context(nc.psum_tensor("psTd", [128, 1024], BF16))
    psTc2 = s_swb.enter_context(nc.psum_tensor("psTc2", [128, 1024], BF16))
    psTd2 = s_swb.enter_context(nc.psum_tensor("psTd2", [128, 1024], BF16))

    wv_ck = emit_w_transpose(wv_stage, WvT, gv, [psTc, psTd, psTc2, psTd2],
                             base=wv_base)

    g_eng.wait_ge(ccs, CC["ccs"])
    g_eng.wait_ge(sms, wv_ck)
    for qi in range(len(KQ_B0)):
        lo, hi = (KQ_B0[qi - 1] if qi else 0), KQ_B0[qi]
        for r in range(2):
            KL_DONE.append(gdma_start(
                K_sb[:, r * HC * QN:(r + 1) * HC * QN]
                    .rearrange("p (e q) -> p e q", e=HC)[:, :, lo * 256:hi * 256],
                kxq_out[qi][r]))

    s_xgb = ExitStack()
    xc3 = [s_xgb.enter_context(nc.sbuf_tensor(f"xcB{i}", [128, D], F32)) for i in range(3)]
    xnB2 = [s_xgb.enter_context(nc.sbuf_tensor(f"xnB{i}", [128, D], BF16)) for i in range(2)]
    xnTB2 = [s_xgb.enter_context(nc.sbuf_tensor(f"xnTB{i}", [128, HC * 256], BF16)) for i in range(2)]
    psV0 = s_xgb.enter_context(nc.psum_tensor("psV0", [128, 512], F32))
    psV1 = s_xgb.enter_context(nc.psum_tensor("psV1", [128, 512], F32))

    st_b = dict(apply={}, lx_at={})
    xnt_evac = {}
    xnt_trans = {}
    v_mm_done = {}
    vevac = {}
    xvc = xb.rearrange("(k p) d -> k p d", p=128)

    ldx3 = [ldx0, ldx1, ldx2]
    lxk3 = ["lx0", "lx1", "lx2"]

    def emit_xload_B(k):
        if k >= 3:
            q_eng.wait_ge(sms, st_b["apply"][k - 3])
        q_eng.dma_start(out=xc3[k % 3][:], in_=xvc[k]).then_inc(ldx3[k % 3], 16)
        C[lxk3[k % 3]] += 16
        st_b["lx_at"][k] = C[lxk3[k % 3]]

    q_eng.wait_ge(sms, wv_ck)
    for k in range(min(3, G)):
        emit_xload_B(k)

    def emit_prep_B(g):
        apls = {}
        for cc in range(2):
            k = 2 * g + cc
            apls[cc] = ln_chunk(xc3[k % 3], 0, xnB2[k % 2],
                                xnt_trans.get(k - 2), ldx3[k % 3], st_b["lx_at"][k])
            st_b["apply"][k] = apls[cc]
            if k + 3 < G:
                emit_xload_B(k + 3)
        for cc in range(2):
            k = 2 * g + cc
            t_eng.wait_ge(sms, apls[cc])
            if k >= 4:
                t_eng.wait_ge(sms, xnt_evac[k - 4])
            ins = None
            pst = [psTc, psTd, psTc2, psTd2][k % 4]
            for hcc in range(HC):
                ins = t_eng.transpose(pst[:, hcc * 128:(hcc + 1) * 128],
                                      xnB2[k % 2][:, hcc * 128:(hcc + 1) * 128], eyesb[:])
            xnt_trans[k] = tinc(ins)
            s_eng.wait_ge(smt, xnt_trans[k])
            if g >= 2 and cc == 0:
                s_eng.wait_ge(smt, v_mm_done[g - 2])
            cp = s_eng.copy(
                xnTB2[g % 2].rearrange("p (h s) -> p h s", h=HC)[:, :, cc * 128:(cc + 1) * 128],
                pst.rearrange("p (h j) -> p h j", h=HC))
            xnt_evac[k] = sinc(cp)

    NVQ = min(4, G // 2)
    VQ_BOUNDS = [((G // 2) * (q + 1)) // NVQ for q in range(NVQ)]

    emit_prep_B(0)
    for g in range(G // 2):
        if g + 1 < G // 2:
            emit_prep_B(g + 1)
        xnTB = xnTB2[g % 2]
        t_eng.wait_ge(sms, xnt_evac[2 * g + 1])
        if g == 0:
            t_eng.wait_ge(sms, wv_ck)
            for _sem, _val in GD_BIASES:
                t_eng.wait_ge(_sem, _val)
            t_eng.wait_ge(ldc, LC_TOTAL)
        psVs = [psV0, psV1]
        for cc in range(2):
            tci = 2 * g + cc
            for half in range(2):
                idx = 2 * cc + half
                if (g, idx - 2) in vevac:
                    t_eng.wait_ge(sms, vevac[(g, idx - 2)])
                elif g >= 1 and idx < 2:
                    t_eng.wait_ge(sms, vevac[(g - 1, 2 + idx)])
                for hcc in range(HC):
                    t_eng.matmul(psVs[idx % 2][:],
                                 xnTB[:, hcc * 256 + cc * 128: hcc * 256 + (cc + 1) * 128],
                                 WvT[:, hcc * D + half * 512: hcc * D + (half + 1) * 512],
                                 start=(hcc == 0), stop=False)
                ins = t_eng.matmul(psVs[idx % 2][:], ones_sb[0:1, :],
                                   bv_sb[0:1, half * 512:(half + 1) * 512],
                                   start=False, stop=True)
                tk = tinc(ins)
                s_eng.wait_ge(smt, tk)
                ev = s_eng.copy(V_sb[:, tci * D + half * 512: tci * D + (half + 1) * 512],
                                psVs[idx % 2][:])
                vevac[(g, idx)] = sinc(ev)
        v_mm_done[g] = C["t"]
        if (g + 1) in VQ_BOUNDS:
            qi = VQ_BOUNDS.index(g + 1)
            lo, hi = (VQ_BOUNDS[qi - 1] if qi else 0), g + 1
            HWq = (hi - lo) * 2 * D
            o0 = lo * 2 * D
            q_eng.wait_ge(sms, vevac[(g, 3)])
            q_eng.dma_start(out=vx_in[qi][0],
                            in_=V_sb[:, o0:o0 + HWq]).then_inc(stxv[qi], 16)
            g_eng.wait_ge(stxv[qi], 16)
            g_eng.collective_compute("AllGather", ALU.bypass,
                                     replica_groups=PAIR_GROUPS,
                                     ins=[vx_in[qi][:]],
                                     outs=[vx_out[qi][:]]).then_inc(ccs, 1)
            CC["ccs"] += 1
            g_eng.wait_ge(ccs, CC["ccs"])
            for r in range(2):
                dst0 = (r * (QN // 128) * D) + o0
                VL_DONE.append(gdma_start(V_sb[:, dst0:dst0 + HWq],
                                          vx_out[qi][r]))

    qkv_done_ck = C["s"]
    s_xgb.close()
    s_swb.close()

    s_at = ExitStack()
    theta = s_at.enter_context(nc.sbuf_tensor("theta", [128, SEQ], F32))
    P_sb = s_at.enter_context(nc.sbuf_tensor("P_sb", [128, SEQ], BF16))
    PT_sb = s_at.enter_context(nc.sbuf_tensor("PT_sb", [128, SEQ], BF16))
    t1a = s_at.enter_context(nc.sbuf_tensor("t1a", [128, 512], BF16))
    t1b = s_at.enter_context(nc.sbuf_tensor("t1b", [128, 512], BF16))
    t2a = s_at.enter_context(nc.sbuf_tensor("t2a", [128, 512], BF16))
    t2b = s_at.enter_context(nc.sbuf_tensor("t2b", [128, 512], BF16))
    OnA = s_at.enter_context(nc.sbuf_tensor("OnA", [128, D], BF16))
    OnB = s_at.enter_context(nc.sbuf_tensor("OnB", [128, D], BF16))
    psS4 = [s_at.enter_context(nc.psum_tensor(f"psS{i}", [128, 512], F32)) for i in range(4)]
    psO2 = [s_at.enter_context(nc.psum_tensor(f"psO{i}", [128, 1024], F32)) for i in range(2)]

    t1s, t2s = [t1a, t1b], [t2a, t2b]
    psSs = psS4
    Ons = [OnA, OnB]
    stOns = [stOn0, stOn1]
    sok = ["so0", "so1"]

    lut_ck = {}
    thw_ck = {}
    score_ck = {}
    pt_ck = {}
    ptev_ck = {}
    av_ck = {}
    oev_ck = {}
    exp_ck = {}
    negm_ck = {}
    recip_ck = {}
    onst_at = {}

    for i in range(TILES + 1):
        j = i - 1
        if j >= 0:
            q_eng.wait_ge(sms, exp_ck[j])
            if j == 0:
                for _sem, _val in dedup_waits(KL_DONE + VL_DONE + wo_pre):
                    q_eng.wait_ge(_sem, _val)
            if j >= 1:
                q_eng.wait_ge(smt, av_ck[j - 1])
            q_eng.dma_start(out=PT_sb.rearrange("p (tc j2) -> p tc j2", tc=TC),
                            in_=P_sb[:, :], transpose=True).then_inc(ldpt, 16)
            C["ldpt"] = C.get("ldpt", 0) + 16
            q_eng.wait_ge(ldpt, C["ldpt"])
        if i < TILES:
            if i == 0:
                t_eng.wait_ge(sms, sweepA_s_ck)
            for w in range(WIN):
                widx = i * WIN + w
                if i == 0 and w == 0:
                    for _sem, _val in dedup_waits(KL_DONE):
                        t_eng.wait_ge(_sem, _val)
                if widx >= 4:
                    pi, pw = divmod(widx - 4, WIN)
                    t_eng.wait_ge(sms, lut_ck[(pi, pw)])
                ins = None
                t0 = w * 512
                kbase = (0 if t0 < QN else HC * QN)
                tl = t0 if t0 < QN else t0 - QN
                for ec in range(HC):
                    ins = t_eng.matmul(
                        psSs[widx % 4][:],
                        Q_sb[:, ec * QN + i * 128: ec * QN + (i + 1) * 128],
                        K_sb[:, kbase + ec * QN + tl: kbase + ec * QN + tl + 512],
                        start=(ec == 0), stop=(ec == HC - 1))
                score_ck[(i, w)] = tinc(ins)
                s_eng.wait_ge(smt, score_ck[(i, w)])
                if widx >= 2:
                    pi, pw = divmod(widx - 2, WIN)
                    s_eng.wait_ge(smv, thw_ck[(pi, pw)])
                s_eng.activation(t1s[widx % 2][:], psSs[widx % 4][:], AF.Tanh, scale=0.5)
                s_eng.activation(t2s[widx % 2][:], psSs[widx % 4][:], AF.Tanh)
                rl = s_eng.activation(theta[:, w * 512:(w + 1) * 512],
                                      psSs[widx % 4][:], AF.Relu, scale=0.1)
                lut_ck[(i, w)] = sinc(rl)
                v_eng.wait_ge(sms, lut_ck[(i, w)])
                if i >= 1 and w == 0:
                    v_eng.wait_ge(sms, exp_ck[i - 1])
                v_eng.scalar_tensor_tensor(t1s[widx % 2][:], t1s[widx % 2][:],
                                           2.0 / 3.0, t2s[widx % 2][:],
                                           ALU.mult, ALU.add)
                v_eng.drain()
                st2 = v_eng.scalar_tensor_tensor(theta[:, w * 512:(w + 1) * 512],
                                                 t1s[widx % 2][:], 0.15,
                                                 theta[:, w * 512:(w + 1) * 512],
                                                 ALU.mult, ALU.add)
                v_eng.drain()
                mx = v_eng.reduce_max(mcol[:, w:w + 1], theta[:, w * 512:(w + 1) * 512],
                                      axis=mybir.AxisListType.X)
                thw_ck[(i, w)] = vinc(mx)

        if j >= 0:
            t_eng.wait_ge(ldpt, 16 * (j + 1))
            if j >= 2:
                t_eng.wait_ge(sms, oev_ck[j - 2])
            ins = None
            if j == 0 and NVQ >= 2:
                bnd = 2 * VQ_BOUNDS[NVQ - 2]
                part1 = list(range(0, bnd)) + list(range(TC // 2, TC // 2 + bnd))
                part2 = [tci for tci in range(TC) if tci not in part1]
                for _sem, _val in dedup_waits(VL_DONE[:2 * (NVQ - 1)]):
                    t_eng.wait_ge(_sem, _val)
                for half in range(2):
                    for tci in part1:
                        t_eng.matmul(
                            psO2[0][:, half * 512:(half + 1) * 512],
                            PT_sb[:, tci * 128:(tci + 1) * 128],
                            V_sb[:, tci * D + half * 512: tci * D + (half + 1) * 512],
                            start=(tci == part1[0]), stop=False)
                for _sem, _val in dedup_waits(VL_DONE):
                    t_eng.wait_ge(_sem, _val)
                for half in range(2):
                    for tci in part2:
                        ins = t_eng.matmul(
                            psO2[0][:, half * 512:(half + 1) * 512],
                            PT_sb[:, tci * 128:(tci + 1) * 128],
                            V_sb[:, tci * D + half * 512: tci * D + (half + 1) * 512],
                            start=False, stop=(tci == part2[-1]))
            else:
                if j == 0:
                    for _sem, _val in dedup_waits(VL_DONE):
                        t_eng.wait_ge(_sem, _val)
                for half in range(2):
                    for tci in range(TC):
                        ins = t_eng.matmul(
                            psO2[j % 2][:, half * 512:(half + 1) * 512],
                            PT_sb[:, tci * 128:(tci + 1) * 128],
                            V_sb[:, tci * D + half * 512: tci * D + (half + 1) * 512],
                            start=(tci == 0), stop=(tci == TC - 1))
            av_ck[j] = tinc(ins)

        if i < TILES:
            v_eng.drain()
            v_eng.reduce_max(mrow[:], mcol[:, 0:WIN], axis=mybir.AxisListType.X)
            v_eng.drain()
            ng = v_eng.tensor_scalar(negm[:], mrow[:], -1.0, None, ALU.mult)
            negm_ck[i] = vinc(ng)
        if j >= 0:
            v_eng.wait_ge(sms, exp_ck[j])
            if j >= 1:
                v_eng.wait_ge(sms, oev_ck[j - 1])
            rc = v_eng.reciprocal(rrow[:], lrow[:])
            recip_ck[j] = vinc(rc)

        if i < TILES:
            if i >= 1:
                s_eng.wait_ge(smv, recip_ck[i - 1])
                s_eng.wait_ge(ldpt, 16 * i)
            else:
                s_eng.wait_ge(smv, negm_ck[0])
            ex = s_eng.activation(P_sb[:], theta[:], AF.Exp,
                                  bias=negm[:], scale=1.0, accum_out=lrow[:])
            exp_ck[i] = sinc(ex)
            s_eng.drain()
        if j >= 0:
            s_eng.wait_ge(smt, av_ck[j])
            s_eng.wait_ge(smv, recip_ck[j])
            if j >= 2:
                s_eng.wait_ge(stOns[j % 2], onst_at[j - 2])
            ev = s_eng.activation(Ons[j % 2][:], psO2[j % 2][:], AF.Identity,
                                  bias=0.0, scale=rrow[:])
            oev_ck[j] = sinc(ev)

        if j >= 0:
            q_eng.wait_ge(sms, oev_ck[j])
            q_eng.dma_start(out=on_scr[j * 128:(j + 1) * 128, :],
                            in_=Ons[j % 2][:]).then_inc(stOns[j % 2], 16)
            C[sok[j % 2]] += 16
            onst_at[j] = C[sok[j % 2]]
            q_eng.wait_ge(stOns[j % 2], onst_at[j])

    attn_end_s = C["s"]
    ST_ON_TOT = (C["so0"], C["so1"])
    s_at.close()
    s_v.close()
    s_kq.close()

    s_f = ExitStack()
    W1_sb = s_f.enter_context(nc.sbuf_tensor("W1_sb", [128, HC * INTER], BF16))
    W2_sb = s_f.enter_context(nc.sbuf_tensor("W2_sb", [128, IC * D], BF16))
    WoT = s_f.enter_context(nc.sbuf_tensor("WoT", [128, HC * D], BF16))
    H_sb = s_f.enter_context(nc.sbuf_tensor("H_sb", [128, IC * 256], BF16))
    x2nT2 = [s_f.enter_context(nc.sbuf_tensor(f"x2nT{i}", [128, HC * 256], BF16)) for i in range(2)]
    x2_2 = [s_f.enter_context(nc.sbuf_tensor(f"x2_{i}", [128, 2 * D], F32)) for i in range(2)]
    xr_sb = s_f.enter_context(nc.sbuf_tensor("xr_sb", [128, D], F32))
    On_sb = s_f.enter_context(nc.sbuf_tensor("On_sb", [128, D], BF16))
    y_sb = s_f.enter_context(nc.sbuf_tensor("y_sb", [128, D], F32))
    OT2 = [s_f.enter_context(nc.sbuf_tensor(f"OT{i}", [128, HC * 128], BF16)) for i in range(2)]
    xnF = s_f.enter_context(nc.sbuf_tensor("xnF", [128, D], BF16))
    psTf0 = s_f.enter_context(nc.psum_tensor("psTf0", [128, 1024], BF16))
    psTf1 = s_f.enter_context(nc.psum_tensor("psTf1", [128, 1024], BF16))
    psY = s_f.enter_context(nc.psum_tensor("psY", [128, 1024], F32))
    psH0 = s_f.enter_context(nc.psum_tensor("psH0", [128, 512], F32))
    psH1 = s_f.enter_context(nc.psum_tensor("psH1", [128, 512], F32))
    psY2 = s_f.enter_context(nc.psum_tensor("psY2", [128, 1024], F32))

    g_eng.wait_ge(sms, attn_end_s)
    g_eng.wait_ge(stOn0, ST_ON_TOT[0])
    g_eng.wait_ge(stOn1, ST_ON_TOT[1])
    for _s, _v in dedup_waits(wo_pre):
        g_eng.wait_ge(_s, _v)
    go = gdma_start(H_sb[:, 0:HC * D].rearrange("p (e d) -> p e d", e=HC),
                    wobf.rearrange("(e p) d -> p e d", p=128))
    w1v = W1.rearrange("(q h p) i -> q p h i", q=2, p=128)
    w1_gd = []
    for qq in range(2):
        w1_gd.append(gdma_start(
            W1_sb[:, qq * 4 * INTER:(qq + 1) * 4 * INTER].rearrange("p (h i) -> p h i", h=4),
            w1v[qq]))
    w2v = W2.rearrange("(q ic p) h -> q p ic h", q=4, p=128)
    w2_gd = []
    for qq in range(4):
        w2_gd.append(gdma_start(
            W2_sb[:, qq * 8 * D:(qq + 1) * 8 * D].rearrange("p (ic h) -> p ic h", ic=8),
            w2v[qq]))
    wo_ck = emit_w_transpose(H_sb, WoT, go, [psTf0, psTf1])

    onv1 = on_scr.rearrange("(k p) d -> k p d", p=128)
    xrv1 = xb.rearrange("(k p) d -> k p d", p=128)

    ontr_ck = {}
    xntr_ck = {}
    otev_ck = {}
    yp_ck = {}
    x2_ck = {}
    x2nt_ck = {}
    h_ck = {}
    hev_ck = {}
    f2_ck = {}
    yv_ck = {}
    lon_at = {}
    lxr_at = {}
    apply_ck2 = {}
    psTfs = [psTf0, psTf1]
    psHs = [psH0, psH1]

    for u in range(FS + 1):
        if u < FS:
            xp = u % 2
            for sg in range(2):
                if (u, sg) == (0, 0):
                    q_eng.wait_ge(stOn0, ST_ON_TOT[0])
                    q_eng.wait_ge(stOn1, ST_ON_TOT[1])
                else:
                    psg = (u, sg - 1) if sg == 1 else (u - 1, 1)
                    q_eng.wait_ge(smt, ontr_ck[psg])
                q_eng.dma_start(out=On_sb[:], in_=onv1[u * 2 + sg]).then_inc(lOn, 16)
                C["lon"] += 16
                lon_at[(u, sg)] = C["lon"]
                t_eng.wait_ge(lOn, lon_at[(u, sg)])
                if u == 0 and sg == 0:
                    t_eng.wait_ge(sms, wo_ck)
                if u >= 1:
                    t_eng.wait_ge(sms, x2nt_ck[(u - 1, sg)])
                ins = None
                pst = psTfs[sg % 2]
                for hcc in range(HC):
                    ins = t_eng.transpose(pst[:, hcc * 128:(hcc + 1) * 128],
                                          On_sb[:, hcc * 128:(hcc + 1) * 128],
                                          eyesb[:])
                ontr_ck[(u, sg)] = tinc(ins)
                s_eng.wait_ge(smt, ontr_ck[(u, sg)])
                if u >= 1:
                    s_eng.wait_ge(smt, yp_ck[(u - 1, sg)])
                cp = s_eng.copy(OT2[sg][:], pst[:])
                otev_ck[(u, sg)] = sinc(cp)

            for sg in range(2):
                if (u, sg) != (0, 0):
                    psg = (u, sg - 1) if sg == 1 else (u - 1, 1)
                    q_eng.wait_ge(smv, x2_ck[psg])
                q_eng.dma_start(out=xr_sb[:], in_=xrv1[u * 2 + sg]).then_inc(lxr, 16)
                C["lxr"] += 16
                lxr_at[(u, sg)] = C["lxr"]
                t_eng.wait_ge(sms, otev_ck[(u, sg)])
                if (u, sg) != (0, 0):
                    psg = (u, sg - 1) if sg == 1 else (u - 1, 1)
                    t_eng.wait_ge(smv, x2_ck[psg])
                ins = None
                for half in range(2):
                    for hcc in range(HC):
                        t_eng.matmul(psY[:, half * 512:(half + 1) * 512],
                                     OT2[sg][:, hcc * 128:(hcc + 1) * 128],
                                     WoT[:, hcc * D + half * 512: hcc * D + (half + 1) * 512],
                                     start=(hcc == 0), stop=False)
                    ins = t_eng.matmul(psY[:, half * 512:(half + 1) * 512],
                                       ones_sb[0:1, :], bo_sb[0:1, half * 512:(half + 1) * 512],
                                       start=False, stop=True)
                yp_ck[(u, sg)] = tinc(ins)

                v_eng.wait_ge(smt, yp_ck[(u, sg)])
                v_eng.wait_ge(lxr, lxr_at[(u, sg)])
                if u >= 2 and sg == 0:
                    v_eng.wait_ge(sms, x2nt_ck[(u - 2, 1)])
                ad = v_eng.tensor_tensor(x2_2[xp][:, sg * D:(sg + 1) * D], psY[:],
                                         xr_sb[:], ALU.add)
                x2_ck[(u, sg)] = vinc(ad)

                if sg == 0:
                    apl = ln_chunk(x2_2[xp], 0, xnF,
                                   xntr_ck.get((u - 1, 1)),
                                   smv, x2_ck[(u, 0)])
                    apply_ck2[(u, 0)] = apl

        def emit_xntr(uu, sg, extra_scalar_wait=None):
            t_eng.wait_ge(sms, apply_ck2[(uu, sg)])
            ins = None
            pst = psTfs[sg % 2]
            for hcc in range(HC):
                ins = t_eng.transpose(pst[:, hcc * 128:(hcc + 1) * 128],
                                      xnF[:, hcc * 128:(hcc + 1) * 128], eyesb[:])
            tk = tinc(ins)
            xntr_ck[(uu, sg)] = tk
            if sg == 0:
                apl = ln_chunk(x2_2[uu % 2], D, xnF, tk, smv, x2_ck[(uu, 1)])
                apply_ck2[(uu, 1)] = apl
            s_eng.wait_ge(smt, tk)
            if extra_scalar_wait:
                s_eng.wait_ge(smt, extra_scalar_wait)
            cp = s_eng.copy(
                x2nT2[uu % 2].rearrange("p (h s) -> p h s", h=HC)[:, :, sg * 128:(sg + 1) * 128],
                pst.rearrange("p (h j) -> p h j", h=HC))
            x2nt_ck[(uu, sg)] = sinc(cp)

        if u == 0:
            emit_xntr(0, 0)
            emit_xntr(0, 1)

        if u >= 1:
            v = u - 1
            vp = v % 2
            t_eng.wait_ge(sms, x2nt_ck[(v, 1)])
            NP = IC // 2
            for icp in range(NP):
                if v == 0 and icp == 0:
                    t_eng.wait_ge(w1_gd[0][0], w1_gd[0][1])
                    t_eng.wait_ge(w1_gd[1][0], w1_gd[1][1])
                if icp >= 2:
                    t_eng.wait_ge(sms, hev_ck[(v, icp - 2)])
                elif v >= 1:
                    t_eng.wait_ge(sms, hev_ck[(v - 1, NP - 2 + icp)])
                ins = None
                for sub in range(2):
                    ic = 2 * icp + sub
                    for hcc in range(HC):
                        ins = t_eng.matmul(psHs[icp % 2][:, sub * 256:(sub + 1) * 256],
                                           W1_sb[:, hcc * INTER + ic * 128: hcc * INTER + (ic + 1) * 128],
                                           x2nT2[vp][:, hcc * 256:(hcc + 1) * 256],
                                           start=(hcc == 0), stop=(hcc == HC - 1))
                h_ck[(v, icp)] = tinc(ins)
                s_eng.wait_ge(smt, h_ck[(v, icp)])
                if v >= 1 and icp == 0:
                    s_eng.wait_ge(smt, f2_ck[(v - 1, 1)])
                ev = None
                for sub in range(2):
                    ic = 2 * icp + sub
                    ev = s_eng.activation(H_sb[:, ic * 256:(ic + 1) * 256],
                                          psHs[icp % 2][:, sub * 256:(sub + 1) * 256], AF.Relu,
                                          bias=b1_sb[:, ic:ic + 1], scale=1.0)
                hev_ck[(v, icp)] = sinc(ev)

            if u < FS:
                emit_xntr(u, 0, extra_scalar_wait=h_ck[(u - 1, IC // 2 - 1)])
                emit_xntr(u, 1)

            for sg in range(2):
                t_eng.wait_ge(sms, hev_ck[(v, IC // 2 - 1)])
                if (v, sg) != (0, 0):
                    psg = (v, sg - 1) if sg == 1 else (v - 1, 1)
                    t_eng.wait_ge(smv, yv_ck[psg])
                ins = None
                for half in range(2):
                    for ic in range(IC):
                        if v == 0 and sg == 0 and half == 0 and ic % 8 == 0:
                            t_eng.wait_ge(w2_gd[ic // 8][0], w2_gd[ic // 8][1])
                        t_eng.matmul(psY2[:, half * 512:(half + 1) * 512],
                                     H_sb[:, ic * 256 + sg * 128: ic * 256 + (sg + 1) * 128],
                                     W2_sb[:, ic * D + half * 512: ic * D + (half + 1) * 512],
                                     start=(ic == 0), stop=False)
                    ins = t_eng.matmul(psY2[:, half * 512:(half + 1) * 512],
                                       ones_sb[0:1, :], b2_sb[0:1, half * 512:(half + 1) * 512],
                                       start=False, stop=True)
                f2_ck[(v, sg)] = tinc(ins)
                v_eng.wait_ge(smt, f2_ck[(v, sg)])
                sidx = 2 * v + sg
                if sidx >= 1:
                    v_eng.wait_ge(sty, 16 * sidx)
                ad = v_eng.tensor_tensor(y_sb[:], psY2[:],
                                         x2_2[vp][:, sg * D:(sg + 1) * D], ALU.add)
                yv_ck[(v, sg)] = vinc(ad)
                s_eng.wait_ge(smv, yv_ck[(v, sg)])
                s_eng.dma_start(out=y[sidx * 128:(sidx + 1) * 128, :],
                                in_=y_sb[:]).then_inc(sty, 16)
                C["sty"] += 16

    q_eng.wait_ge(sty, C["sty"])

    s_f.close()
    top.close()
    return nc


_NC_CACHE = {}


def get_nc(SEQ=4096):
    if SEQ not in _NC_CACHE:
        _NC_CACHE[SEQ] = build_nc(SEQ)
    return _NC_CACHE[SEQ]


def core_inputs(x_full, weights, SEQ=4096):
    B = x_full.shape[0]
    eye_np = np.eye(128, dtype=BF)
    ones_np = np.ones((1, 128), dtype=BF)
    QN = SEQ // 2
    in_maps = []
    for c in range(2 * B):
        b, h = divmod(c, 2)
        xbat = x_full[b]
        xbp = np.ascontiguousarray(
            np.concatenate([xbat[h * QN:(h + 1) * QN], xbat[(1 - h) * QN:(2 - h) * QN]], axis=0))
        m = dict(weights)
        m["xb"] = xbp
        m["eye"] = eye_np
        m["ones"] = ones_np
        in_maps.append(m)
    return in_maps


def kernel(L0_attn_x, L0_attn_Wq, L0_attn_bq, L0_attn_Wk, L0_attn_bk,
           L0_attn_Wv, L0_attn_bv, L0_attn_Wo, L0_attn_bo,
           L0_ffn_W1, L0_ffn_b1, L0_ffn_W2, L0_ffn_b2):
    SEQ = 4096
    QN = SEQ // 2
    x = np.ascontiguousarray(np.asarray(L0_attn_x, dtype=np.float32))
    weights = dict(
        Wq=np.ascontiguousarray(np.asarray(L0_attn_Wq, dtype=np.float32)),
        Wk=np.ascontiguousarray(np.asarray(L0_attn_Wk, dtype=np.float32)),
        Wv=np.ascontiguousarray(np.asarray(L0_attn_Wv, dtype=np.float32)),
        Wo=np.ascontiguousarray(np.asarray(L0_attn_Wo, dtype=np.float32)),
        bq=np.ascontiguousarray(np.asarray(L0_attn_bq, dtype=np.float32)),
        bk=np.ascontiguousarray(np.asarray(L0_attn_bk, dtype=np.float32)),
        bv=np.ascontiguousarray(np.asarray(L0_attn_bv, dtype=np.float32)),
        bo=np.ascontiguousarray(np.asarray(L0_attn_bo, dtype=np.float32)),
        W1=np.ascontiguousarray(np.asarray(L0_ffn_W1, dtype=np.float32)),
        b1=np.ascontiguousarray(np.asarray(L0_ffn_b1, dtype=np.float32)),
        W2=np.ascontiguousarray(np.asarray(L0_ffn_W2, dtype=np.float32)),
        b2=np.ascontiguousarray(np.asarray(L0_ffn_b2, dtype=np.float32)),
    )
    nc = get_nc(SEQ)
    in_maps = core_inputs(x, weights, SEQ)
    res = run_bass_kernel_spmd(nc, in_maps, core_ids=list(range(8)))
    out = np.empty((4, SEQ, 1024), dtype=np.float32)
    for c in range(8):
        b, h = divmod(c, 2)
        out[b, h * QN:(h + 1) * QN] = res.results[c]["y"]
    return out


# version 32 (final)
# speedup vs baseline: 1.1950x; 1.0894x over previous
import numpy as np
import ml_dtypes
from contextlib import ExitStack

import concourse.bass as bass
import concourse.mybir as mybir
from concourse.bass_utils import run_bass_kernel_spmd
from concourse import library_config

F32 = mybir.dt.float32
BF16 = mybir.dt.bfloat16
AF = mybir.ActivationFunctionType
ALU = mybir.AluOpType
BF = ml_dtypes.bfloat16

D = 1024
HC = 8
INTER = 4096
IC = 32
EPS = 1e-5


def build_nc(SEQ=4096):
    QN = SEQ // 2
    G = SEQ // 256
    GQ = G // 2
    TILES = QN // 128
    WIN = SEQ // 512
    TC = SEQ // 128
    PB = TC // 8
    FS = QN // 256

    nc = bass.Bass("TRN2", target_bir_lowering=False)
    xb = nc.dram_tensor("xb", [SEQ, D], F32, kind="ExternalInput")
    Wq = nc.dram_tensor("Wq", [D, D], F32, kind="ExternalInput")
    Wk = nc.dram_tensor("Wk", [D, D], F32, kind="ExternalInput")
    Wv = nc.dram_tensor("Wv", [D, D], F32, kind="ExternalInput")
    Wo = nc.dram_tensor("Wo", [D, D], F32, kind="ExternalInput")
    bq = nc.dram_tensor("bq", [D], F32, kind="ExternalInput")
    bk = nc.dram_tensor("bk", [D], F32, kind="ExternalInput")
    bv = nc.dram_tensor("bv", [D], F32, kind="ExternalInput")
    bo = nc.dram_tensor("bo", [D], F32, kind="ExternalInput")
    W1 = nc.dram_tensor("W1", [D, INTER], F32, kind="ExternalInput")
    b1 = nc.dram_tensor("b1", [INTER], F32, kind="ExternalInput")
    W2 = nc.dram_tensor("W2", [INTER, D], F32, kind="ExternalInput")
    b2 = nc.dram_tensor("b2", [D], F32, kind="ExternalInput")
    eye = nc.dram_tensor("eye", [128, 128], BF16, kind="ExternalInput")
    ones = nc.dram_tensor("ones", [1, 128], BF16, kind="ExternalInput")
    y = nc.dram_tensor("y", [QN, D], F32, kind="ExternalOutput")
    on_scr = nc.dram_tensor("on_scr", [QN, D], BF16, kind="Internal")
    NVQ0 = min(4, SEQ // 512)
    KQ_B0 = [((SEQ // 512) * (q + 1)) // NVQ0 for q in range(NVQ0)]
    kxq_in = [nc.dram_tensor(f"kxq_in{i}", [1, 128, HC * (KQ_B0[i] - (KQ_B0[i - 1] if i else 0)) * 256], BF16, kind="Internal") for i in range(NVQ0)]
    kxq_out = [nc.dram_tensor(f"kxq_out{i}", [2, 128, HC * (KQ_B0[i] - (KQ_B0[i - 1] if i else 0)) * 256], BF16, kind="Internal") for i in range(NVQ0)]
    QCOLS = ((SEQ // 512) // NVQ0) * 2 * D
    vx_in = [nc.dram_tensor(f"vx_in{i}", [1, 128, QCOLS], BF16, kind="Internal") for i in range(NVQ0)]
    vx_out = [nc.dram_tensor(f"vx_out{i}", [2, 128, QCOLS], BF16, kind="Internal") for i in range(NVQ0)]

    t_eng = nc.tensor
    s_eng = nc.scalar
    v_eng = nc.vector
    g_eng = nc.gpsimd
    q_eng = nc.sync

    top = ExitStack()
    E = top.enter_context

    gdma = E(nc.semaphore("gdma"))
    gdmas = [gdma] + [E(nc.semaphore(f"gdma{i}")) for i in range(1, 4)]
    ldc = E(nc.semaphore("ldc"))
    ldx0 = E(nc.semaphore("ldx0"))
    ldx1 = E(nc.semaphore("ldx1"))
    ldx2 = E(nc.semaphore("ldx2"))
    smt = E(nc.semaphore("smt"))
    sms = E(nc.semaphore("sms"))
    smv = E(nc.semaphore("smv"))
    stOn0 = E(nc.semaphore("stOn0"))
    stOn1 = E(nc.semaphore("stOn1"))
    lOn = E(nc.semaphore("lOn"))
    lxr = E(nc.semaphore("lxr"))
    sty = E(nc.semaphore("sty"))
    stw = E(nc.semaphore("stw"))
    ldwo = E(nc.semaphore("ldwo"))
    ldw1 = E(nc.semaphore("ldw1"))
    ldw2 = E(nc.semaphore("ldw2"))
    ldpt = E(nc.semaphore("ldpt"))
    ccs = E(nc.semaphore("ccs"))
    stxk = [E(nc.semaphore(f"stxk{i}")) for i in range(4)]
    stxv = [E(nc.semaphore(f"stxv{i}")) for i in range(4)]

    C = dict(gd=0, lc=0, lx0=0, lx1=0, lx2=0, t=0, s=0, v=0, so0=0, so1=0,
             lon=0, lxr=0, sty=0)

    def tinc(ins):
        ins.then_inc(smt, 1); C["t"] += 1; return C["t"]

    def sinc(ins):
        ins.then_inc(sms, 1); C["s"] += 1; return C["s"]

    def vinc(ins):
        ins.then_inc(smv, 1); C["v"] += 1; return C["v"]

    GDC = [0, 0, 0, 0]
    GDN = [0]

    def gdma_start(out, in_):
        i = GDN[0] % 4
        GDN[0] += 1
        if GDC[i] > 0:
            g_eng.wait_ge(gdmas[i], GDC[i])
        g_eng.dma_start(out=out, in_=in_).then_inc(gdmas[i], 16)
        GDC[i] += 16
        return (gdmas[i], GDC[i])

    eyesb = E(nc.sbuf_tensor("eyesb", [128, 128], BF16))
    ones_sb = E(nc.sbuf_tensor("ones_sb", [1, 128], BF16))
    bq_sb = E(nc.sbuf_tensor("bq_sb", [128, HC], F32))
    bk_sb = E(nc.sbuf_tensor("bk_sb", [128, HC], F32))
    b1_sb = E(nc.sbuf_tensor("b1_sb", [128, IC], F32))
    bv_sb = E(nc.sbuf_tensor("bv_sb", [1, D], BF16))
    bo_sb = E(nc.sbuf_tensor("bo_sb", [1, D], BF16))
    b2_sb = E(nc.sbuf_tensor("b2_sb", [1, D], BF16))
    stats2 = [E(nc.sbuf_tensor(f"stats{i}", [128, 12], F32)) for i in range(2)]
    mvt2 = [E(nc.sbuf_tensor(f"mvt{i}", [128, 2], F32)) for i in range(2)]
    rstd2 = [E(nc.sbuf_tensor(f"rstd{i}", [128, 1], F32)) for i in range(2)]
    nmr2 = [E(nc.sbuf_tensor(f"nmr{i}", [128, 1], F32)) for i in range(2)]
    mrow = E(nc.sbuf_tensor("mrow", [128, 1], F32))
    negm = E(nc.sbuf_tensor("negm", [128, 1], F32))
    lrow = E(nc.sbuf_tensor("lrow", [128, 1], F32))
    rrow = E(nc.sbuf_tensor("rrow", [128, 1], F32))
    mcol = E(nc.sbuf_tensor("mcol", [128, 16], F32))

    q_eng.dma_start(out=eyesb[:], in_=eye[:, :]).then_inc(ldc, 16); C["lc"] += 16
    q_eng.dma_start(out=ones_sb[:], in_=ones[:, :]).then_inc(ldc, 16); C["lc"] += 16
    q_eng.dma_start(out=bq_sb[:], in_=bq.rearrange("(e p) -> p e", p=128), allow_slow_non_contiguous=True).then_inc(ldc, 16); C["lc"] += 16
    q_eng.dma_start(out=bk_sb[:], in_=bk.rearrange("(e p) -> p e", p=128), allow_slow_non_contiguous=True).then_inc(ldc, 16); C["lc"] += 16
    q_eng.dma_start(out=b1_sb[:], in_=b1.rearrange("(i p) -> p i", p=128), allow_slow_non_contiguous=True).then_inc(ldc, 16); C["lc"] += 16
    LC_TOTAL = C["lc"]

    GD_BIASES = [gdma_start(bv_sb[:], bv[None, :]),
                 gdma_start(bo_sb[:], bo[None, :]),
                 gdma_start(b2_sb[:], b2[None, :])]

    def emit_w_transpose(wtmp_sb, WT_sb, gd_done, pss, base=0, prev_ck=0):
        evac_ck = {}
        nb = len(pss)
        for ec in range(HC):
            if ec == 0:
                t_eng.wait_ge(gd_done[0], gd_done[1])
                t_eng.wait_ge(ldc, LC_TOTAL)
                if prev_ck:
                    t_eng.wait_ge(sms, prev_ck)
            if ec >= nb:
                t_eng.wait_ge(sms, evac_ck[ec - nb])
            ins = None
            for hcc in range(HC):
                ins = t_eng.transpose(pss[ec % nb][:, hcc * 128:(hcc + 1) * 128],
                                      wtmp_sb[:, base + ec * D + hcc * 128: base + ec * D + (hcc + 1) * 128],
                                      eyesb[:])
            tk = tinc(ins)
            s_eng.wait_ge(smt, tk)
            cp = s_eng.copy(WT_sb.rearrange("p (h e) -> p h e", h=HC)[:, :, ec * 128:(ec + 1) * 128],
                            pss[ec % nb].rearrange("p (h j) -> p h j", h=HC))
            evac_ck[ec] = sinc(cp)
        return evac_ck[HC - 1]

    def emit_ln_chunk(xg_sb, col0, xn_sb, prev2_apply_ck, xn_free_t_ck,
                      ld_sem=None, ld_val=0, par=0):
        stats, mvt, rstd, nmr = stats2[par], mvt2[par], rstd2[par], nmr2[par]
        if ld_sem is not None:
            v_eng.wait_ge(ld_sem, ld_val)
        if prev2_apply_ck:
            v_eng.wait_ge(sms, prev2_apply_ck)
        v_eng.bn_stats(stats[:, 0:6], xg_sb[:, col0:col0 + 512])
        v_eng.bn_stats(stats[:, 6:12], xg_sb[:, col0 + 512:col0 + D])
        v_eng.drain()
        v_eng.bn_aggr(mvt[:], stats[:])
        v_eng.drain()
        v_eng.tensor_scalar(rstd[:], mvt[:, 1:2], EPS, None, ALU.add)
        v_eng.drain()
        v_eng.reciprocal(rstd[:], rstd[:])
        ts = v_eng.tensor_scalar(nmr[:], mvt[:, 0:1], -1.0, None, ALU.mult)
        v1 = vinc(ts)
        s_eng.wait_ge(smv, v1)
        s_eng.activation(rstd[:], rstd[:], AF.Sqrt)
        s_eng.drain()
        s_eng.activation(nmr[:], nmr[:], AF.Identity, scale=rstd[:])
        s_eng.drain()
        if xn_free_t_ck:
            s_eng.wait_ge(smt, xn_free_t_ck)
        ap = s_eng.activation(xn_sb[:], xg_sb[:, col0:col0 + D], AF.Identity,
                              bias=nmr[:], scale=rstd[:])
        return sinc(ap)

    PAIR_GROUPS = [[0, 1], [2, 3], [4, 5], [6, 7]]
    CC = dict(ccs=0)
    KL_DONE = []
    VL_DONE = []

    def dedup_waits(tups):
        m = {}
        for sem, val in tups:
            if id(sem) not in m or m[id(sem)][1] < val:
                m[id(sem)] = (sem, val)
        return list(m.values())

    ln_last = {0: 0, 1: 0}
    ln_k = [0]

    def ln_chunk(xg_sb, col0, xn_sb, xn_free_t_ck, ld_sem=None, ld_val=0):
        par = ln_k[0] % 2
        ck = emit_ln_chunk(xg_sb, col0, xn_sb, ln_last[par], xn_free_t_ck,
                           ld_sem, ld_val, par=par)
        ln_last[par] = ck
        ln_k[0] += 1
        return ck

    s_kq = ExitStack()
    K_sb = s_kq.enter_context(nc.sbuf_tensor("K_sb", [128, HC * SEQ], BF16))
    Q_sb = s_kq.enter_context(nc.sbuf_tensor("Q_sb", [128, HC * QN], BF16))
    if HC * QN >= HC * D:
        wv_stage, wv_base = K_sb, HC * QN
    else:
        wv_stage = s_kq.enter_context(nc.sbuf_tensor("wvstg", [128, HC * D], BF16))
        wv_base = 0

    s_swa = ExitStack()
    WqT = s_swa.enter_context(nc.sbuf_tensor("WqT", [128, HC * D], BF16))
    WkT = s_swa.enter_context(nc.sbuf_tensor("WkT", [128, HC * D], BF16))
    psTa = s_swa.enter_context(nc.psum_tensor("psTa", [128, 1024], BF16))
    psTb = s_swa.enter_context(nc.psum_tensor("psTb", [128, 1024], BF16))
    psTa2 = s_swa.enter_context(nc.psum_tensor("psTa2", [128, 1024], BF16))
    psTb2 = s_swa.enter_context(nc.psum_tensor("psTb2", [128, 1024], BF16))

    s_wtmp = ExitStack()
    wtmpA = s_wtmp.enter_context(nc.sbuf_tensor("wtmpA", [128, HC * D], BF16))
    wtmpA2 = s_wtmp.enter_context(nc.sbuf_tensor("wtmpA2", [128, HC * D], BF16))
    gq = gdma_start(wtmpA.rearrange("p (e d) -> p e d", e=HC),
                    Wq.rearrange("(e p) d -> p e d", p=128))
    g_eng.wait_ge(gq[0], gq[1])
    gk = gdma_start(wtmpA2.rearrange("p (e d) -> p e d", e=HC),
                    Wk.rearrange("(e p) d -> p e d", p=128))
    g_eng.wait_ge(gk[0], gk[1])
    gv = gdma_start(wv_stage[:, wv_base:wv_base + HC * D].rearrange("p (e d) -> p e d", e=HC),
                    Wv.rearrange("(e p) d -> p e d", p=128))
    wq_ck = emit_w_transpose(wtmpA, WqT, gq, [psTa, psTb, psTa2, psTb2])
    wk_ck = emit_w_transpose(wtmpA2, WkT, gk, [psTa, psTb, psTa2, psTb2],
                             prev_ck=wq_ck)

    s_wtmp.close()

    s_xga = ExitStack()
    xgA0 = s_xga.enter_context(nc.sbuf_tensor("xgA0", [128, 2 * D], F32))
    xgA1 = s_xga.enter_context(nc.sbuf_tensor("xgA1", [128, 2 * D], F32))
    xnA2 = [s_xga.enter_context(nc.sbuf_tensor(f"xnA{i}", [128, D], BF16)) for i in range(2)]
    xnTA2 = [s_xga.enter_context(nc.sbuf_tensor(f"xnTA{i}", [128, HC * 256], BF16)) for i in range(2)]
    psK0 = s_xga.enter_context(nc.psum_tensor("psK0", [128, 512], F32))
    psK1 = s_xga.enter_context(nc.psum_tensor("psK1", [128, 512], F32))
    psQ0 = s_xga.enter_context(nc.psum_tensor("psQ0", [128, 512], F32))
    psQ1 = s_xga.enter_context(nc.psum_tensor("psQ1", [128, 512], F32))

    xgs = [xgA0, xgA1]
    ldxs = [ldx0, ldx1]
    lxk = ["lx0", "lx1"]
    xv = xb.rearrange("(g c p) d -> g p c d", p=128, c=2)

    st_a = dict(xgfree={}, lx_at={})

    def emit_xload(g, state):
        par = g % 2
        if g >= 2:
            q_eng.wait_ge(sms, state["xgfree"][g - 2])
        q_eng.dma_start(out=xgs[par].rearrange("p (c d) -> p c d", c=2),
                        in_=xv[g]).then_inc(ldxs[par], 16)
        C[lxk[par]] += 16
        state["lx_at"][g] = C[lxk[par]]

    q_eng.wait_ge(sms, wq_ck)
    emit_xload(0, st_a)
    if G > 1:
        emit_xload(1, st_a)

    apply_ck = 0
    xnt_evac = {}
    xnt_trans = {}
    kq_mm_done = {}
    kevac = {}
    qevac = {}

    def emit_prep_A(g):
        par = g % 2
        apls = {}
        for cc in range(2):
            k = 2 * g + cc
            apls[cc] = ln_chunk(xgs[par], cc * D, xnA2[k % 2],
                                xnt_trans.get(k - 2), ldxs[par], st_a["lx_at"][g])
            if cc == 1:
                st_a["xgfree"][g] = apls[cc]
                if g + 2 < G // 2:
                    emit_xload(g + 2, st_a)

        for cc in range(2):
            k = 2 * g + cc
            t_eng.wait_ge(sms, apls[cc])
            if k >= 4:
                t_eng.wait_ge(sms, xnt_evac[k - 4])
            ins = None
            pst = [psTa, psTb, psTa2, psTb2][k % 4]
            for hcc in range(HC):
                ins = t_eng.transpose(pst[:, hcc * 128:(hcc + 1) * 128],
                                      xnA2[k % 2][:, hcc * 128:(hcc + 1) * 128], eyesb[:])
            xnt_trans[k] = tinc(ins)
            s_eng.wait_ge(smt, xnt_trans[k])
            if g >= 2 and cc == 0:
                s_eng.wait_ge(smt, kq_mm_done[g - 2])
            cp = s_eng.copy(
                xnTA2[g % 2].rearrange("p (h s) -> p h s", h=HC)[:, :, cc * 128:(cc + 1) * 128],
                pst.rearrange("p (h j) -> p h j", h=HC))
            xnt_evac[k] = sinc(cp)

    G2 = G // 2
    emit_prep_A(0)
    for g in range(G2):
        if g + 1 < G2:
            emit_prep_A(g + 1)
        xnTA = xnTA2[g % 2]
        t_eng.wait_ge(sms, xnt_evac[2 * g + 1])
        if g == 0:
            t_eng.wait_ge(sms, wk_ck)
        psKs = [psK0, psK1]
        for ecp in range(HC // 2):
            if (g, ecp - 2) in kevac:
                t_eng.wait_ge(sms, kevac[(g, ecp - 2)])
            elif g >= 1 and ecp < 2:
                t_eng.wait_ge(sms, kevac[(g - 1, HC // 2 - 2 + ecp)])
            ins = None
            for sub in range(2):
                ec = 2 * ecp + sub
                for hcc in range(HC):
                    ins = t_eng.matmul(psKs[ecp % 2][:, sub * 256:(sub + 1) * 256],
                                       WkT[:, hcc * D + ec * 128: hcc * D + (ec + 1) * 128],
                                       xnTA[:, hcc * 256:(hcc + 1) * 256],
                                       start=(hcc == 0), stop=(hcc == HC - 1))
            tk = tinc(ins)
            s_eng.wait_ge(smt, tk)
            ev = None
            for sub in range(2):
                ec = 2 * ecp + sub
                ev = s_eng.activation(K_sb[:, ec * QN + g * 256: ec * QN + (g + 1) * 256],
                                      psKs[ecp % 2][:, sub * 256:(sub + 1) * 256], AF.Identity,
                                      bias=bk_sb[:, ec:ec + 1], scale=1.0)
            kevac[(g, ecp)] = sinc(ev)
        if True:
            if g == 0:
                t_eng.wait_ge(sms, wq_ck)
            psQs = [psQ0, psQ1]
            for ecp in range(HC // 2):
                if (g, ecp - 2) in qevac:
                    t_eng.wait_ge(sms, qevac[(g, ecp - 2)])
                elif g >= 1 and ecp < 2:
                    t_eng.wait_ge(sms, qevac[(g - 1, HC // 2 - 2 + ecp)])
                ins = None
                for sub in range(2):
                    ec = 2 * ecp + sub
                    for hcc in range(HC):
                        ins = t_eng.matmul(psQs[ecp % 2][:, sub * 256:(sub + 1) * 256],
                                           WqT[:, hcc * D + ec * 128: hcc * D + (ec + 1) * 128],
                                           xnTA[:, hcc * 256:(hcc + 1) * 256],
                                           start=(hcc == 0), stop=(hcc == HC - 1))
                tk = tinc(ins)
                s_eng.wait_ge(smt, tk)
                ev = None
                for sub in range(2):
                    ec = 2 * ecp + sub
                    ev = s_eng.activation(Q_sb[:, ec * QN + g * 256: ec * QN + (g + 1) * 256],
                                          psQs[ecp % 2][:, sub * 256:(sub + 1) * 256], AF.Identity,
                                          bias=bq_sb[:, ec:ec + 1], scale=1.0)
                qevac[(g, ecp)] = sinc(ev)
        kq_mm_done[g] = C["t"]
        if (g + 1) in KQ_B0:
            qi = KQ_B0.index(g + 1)
            lo, hi = (KQ_B0[qi - 1] if qi else 0), g + 1
            q_eng.wait_ge(sms, kevac[(g, HC // 2 - 1)])
            q_eng.dma_start(
                out=kxq_in[qi][0],
                in_=K_sb[:, 0:HC * QN].rearrange("p (e q) -> p e q", e=HC)[:, :, lo * 256:hi * 256],
            ).then_inc(stxk[qi], 16)
            g_eng.wait_ge(stxk[qi], 16)
            g_eng.collective_compute("AllGather", ALU.bypass,
                                     replica_groups=PAIR_GROUPS,
                                     ins=[kxq_in[qi][:]],
                                     outs=[kxq_out[qi][:]]).then_inc(ccs, 1)
            CC["ccs"] += 1

    sweepA_s_ck = C["s"]
    s_xga.close()
    s_swa.close()

    s_v = ExitStack()
    V_sb = s_v.enter_context(nc.sbuf_tensor("V_sb", [128, TC * D], BF16))

    s_swb = ExitStack()
    WvT = s_swb.enter_context(nc.sbuf_tensor("WvT", [128, HC * D], BF16))
    psTc = s_swb.enter_context(nc.psum_tensor("psTc", [128, 1024], BF16))
    psTd = s_swb.enter_context(nc.psum_tensor("psTd", [128, 1024], BF16))
    psTc2 = s_swb.enter_context(nc.psum_tensor("psTc2", [128, 1024], BF16))
    psTd2 = s_swb.enter_context(nc.psum_tensor("psTd2", [128, 1024], BF16))

    wv_ck = emit_w_transpose(wv_stage, WvT, gv, [psTc, psTd, psTc2, psTd2],
                             base=wv_base)

    g_eng.wait_ge(ccs, CC["ccs"])
    g_eng.wait_ge(sms, wv_ck)
    for qi in range(len(KQ_B0)):
        lo, hi = (KQ_B0[qi - 1] if qi else 0), KQ_B0[qi]
        for r in range(2):
            KL_DONE.append(gdma_start(
                K_sb[:, r * HC * QN:(r + 1) * HC * QN]
                    .rearrange("p (e q) -> p e q", e=HC)[:, :, lo * 256:hi * 256],
                kxq_out[qi][r]))

    s_xgb = ExitStack()
    xc3 = [s_xgb.enter_context(nc.sbuf_tensor(f"xcB{i}", [128, D], F32)) for i in range(3)]
    xnB2 = [s_xgb.enter_context(nc.sbuf_tensor(f"xnB{i}", [128, D], BF16)) for i in range(2)]
    xnTB2 = [s_xgb.enter_context(nc.sbuf_tensor(f"xnTB{i}", [128, HC * 256], BF16)) for i in range(2)]
    psV0 = s_xgb.enter_context(nc.psum_tensor("psV0", [128, 512], F32))
    psV1 = s_xgb.enter_context(nc.psum_tensor("psV1", [128, 512], F32))

    st_b = dict(apply={}, lx_at={})
    xnt_evac = {}
    xnt_trans = {}
    v_mm_done = {}
    vevac = {}
    xvc = xb.rearrange("(k p) d -> k p d", p=128)

    ldx3 = [ldx0, ldx1, ldx2]
    lxk3 = ["lx0", "lx1", "lx2"]

    def emit_xload_B(k):
        if k >= 3:
            q_eng.wait_ge(sms, st_b["apply"][k - 3])
        q_eng.dma_start(out=xc3[k % 3][:], in_=xvc[k]).then_inc(ldx3[k % 3], 16)
        C[lxk3[k % 3]] += 16
        st_b["lx_at"][k] = C[lxk3[k % 3]]

    q_eng.wait_ge(smt, kq_mm_done[G2 - 1])
    for k in range(min(3, G)):
        emit_xload_B(k)

    def emit_prep_B(g):
        apls = {}
        for cc in range(2):
            k = 2 * g + cc
            apls[cc] = ln_chunk(xc3[k % 3], 0, xnB2[k % 2],
                                xnt_trans.get(k - 2), ldx3[k % 3], st_b["lx_at"][k])
            st_b["apply"][k] = apls[cc]
            if k + 3 < G:
                emit_xload_B(k + 3)
        for cc in range(2):
            k = 2 * g + cc
            t_eng.wait_ge(sms, apls[cc])
            if k >= 4:
                t_eng.wait_ge(sms, xnt_evac[k - 4])
            ins = None
            pst = [psTc, psTd, psTc2, psTd2][k % 4]
            for hcc in range(HC):
                ins = t_eng.transpose(pst[:, hcc * 128:(hcc + 1) * 128],
                                      xnB2[k % 2][:, hcc * 128:(hcc + 1) * 128], eyesb[:])
            xnt_trans[k] = tinc(ins)
            s_eng.wait_ge(smt, xnt_trans[k])
            if g >= 2 and cc == 0:
                s_eng.wait_ge(smt, v_mm_done[g - 2])
            cp = s_eng.copy(
                xnTB2[g % 2].rearrange("p (h s) -> p h s", h=HC)[:, :, cc * 128:(cc + 1) * 128],
                pst.rearrange("p (h j) -> p h j", h=HC))
            xnt_evac[k] = sinc(cp)

    NVQ = min(4, G // 2)
    VQ_BOUNDS = [((G // 2) * (q + 1)) // NVQ for q in range(NVQ)]

    emit_prep_B(0)
    for g in range(G // 2):
        if g + 1 < G // 2:
            emit_prep_B(g + 1)
        xnTB = xnTB2[g % 2]
        t_eng.wait_ge(sms, xnt_evac[2 * g + 1])
        if g == 0:
            t_eng.wait_ge(sms, wv_ck)
            for _sem, _val in GD_BIASES:
                t_eng.wait_ge(_sem, _val)
            t_eng.wait_ge(ldc, LC_TOTAL)
        psVs = [psV0, psV1]
        for cc in range(2):
            tci = 2 * g + cc
            for half in range(2):
                idx = 2 * cc + half
                if (g, idx - 2) in vevac:
                    t_eng.wait_ge(sms, vevac[(g, idx - 2)])
                elif g >= 1 and idx < 2:
                    t_eng.wait_ge(sms, vevac[(g - 1, 2 + idx)])
                for hcc in range(HC):
                    t_eng.matmul(psVs[idx % 2][:],
                                 xnTB[:, hcc * 256 + cc * 128: hcc * 256 + (cc + 1) * 128],
                                 WvT[:, hcc * D + half * 512: hcc * D + (half + 1) * 512],
                                 start=(hcc == 0), stop=False)
                ins = t_eng.matmul(psVs[idx % 2][:], ones_sb[0:1, :],
                                   bv_sb[0:1, half * 512:(half + 1) * 512],
                                   start=False, stop=True)
                tk = tinc(ins)
                s_eng.wait_ge(smt, tk)
                ev = s_eng.copy(V_sb[:, tci * D + half * 512: tci * D + (half + 1) * 512],
                                psVs[idx % 2][:])
                vevac[(g, idx)] = sinc(ev)
        v_mm_done[g] = C["t"]
        if (g + 1) in VQ_BOUNDS:
            qi = VQ_BOUNDS.index(g + 1)
            lo, hi = (VQ_BOUNDS[qi - 1] if qi else 0), g + 1
            HWq = (hi - lo) * 2 * D
            o0 = lo * 2 * D
            q_eng.wait_ge(sms, vevac[(g, 3)])
            q_eng.dma_start(out=vx_in[qi][0],
                            in_=V_sb[:, o0:o0 + HWq]).then_inc(stxv[qi], 16)
            g_eng.wait_ge(stxv[qi], 16)
            g_eng.collective_compute("AllGather", ALU.bypass,
                                     replica_groups=PAIR_GROUPS,
                                     ins=[vx_in[qi][:]],
                                     outs=[vx_out[qi][:]]).then_inc(ccs, 1)
            CC["ccs"] += 1
            g_eng.wait_ge(ccs, CC["ccs"])
            for r in range(2):
                dst0 = (r * (QN // 128) * D) + o0
                VL_DONE.append(gdma_start(V_sb[:, dst0:dst0 + HWq],
                                          vx_out[qi][r]))

    qkv_done_ck = C["s"]
    s_xgb.close()
    s_swb.close()

    s_at = ExitStack()
    theta = s_at.enter_context(nc.sbuf_tensor("theta", [128, SEQ], F32))
    P_sb = s_at.enter_context(nc.sbuf_tensor("P_sb", [128, SEQ], BF16))
    PT_sb = s_at.enter_context(nc.sbuf_tensor("PT_sb", [128, SEQ], BF16))
    t1a = s_at.enter_context(nc.sbuf_tensor("t1a", [128, 512], BF16))
    t1b = s_at.enter_context(nc.sbuf_tensor("t1b", [128, 512], BF16))
    t2a = s_at.enter_context(nc.sbuf_tensor("t2a", [128, 512], BF16))
    t2b = s_at.enter_context(nc.sbuf_tensor("t2b", [128, 512], BF16))
    OnA = s_at.enter_context(nc.sbuf_tensor("OnA", [128, D], BF16))
    OnB = s_at.enter_context(nc.sbuf_tensor("OnB", [128, D], BF16))
    psS4 = [s_at.enter_context(nc.psum_tensor(f"psS{i}", [128, 512], F32)) for i in range(4)]
    psO2 = [s_at.enter_context(nc.psum_tensor(f"psO{i}", [128, 1024], F32)) for i in range(2)]

    t1s, t2s = [t1a, t1b], [t2a, t2b]
    psSs = psS4
    Ons = [OnA, OnB]
    stOns = [stOn0, stOn1]
    sok = ["so0", "so1"]

    lut_ck = {}
    thw_ck = {}
    score_ck = {}
    pt_ck = {}
    ptev_ck = {}
    av_ck = {}
    oev_ck = {}
    exp_ck = {}
    negm_ck = {}
    recip_ck = {}
    onst_at = {}

    for i in range(TILES + 1):
        j = i - 1
        if j >= 0:
            q_eng.wait_ge(sms, exp_ck[j])
            if j == 0:
                for _sem, _val in dedup_waits(KL_DONE + VL_DONE):
                    q_eng.wait_ge(_sem, _val)
            if j >= 1:
                q_eng.wait_ge(smt, av_ck[j - 1])
            q_eng.dma_start(out=PT_sb.rearrange("p (tc j2) -> p tc j2", tc=TC),
                            in_=P_sb[:, :], transpose=True).then_inc(ldpt, 16)
            C["ldpt"] = C.get("ldpt", 0) + 16
            q_eng.wait_ge(ldpt, C["ldpt"])
        if i < TILES:
            if i == 0:
                t_eng.wait_ge(sms, sweepA_s_ck)
            for w in range(WIN):
                widx = i * WIN + w
                if i == 0 and w == 0:
                    for _sem, _val in dedup_waits(KL_DONE):
                        t_eng.wait_ge(_sem, _val)
                if widx >= 4:
                    pi, pw = divmod(widx - 4, WIN)
                    t_eng.wait_ge(sms, lut_ck[(pi, pw)])
                ins = None
                t0 = w * 512
                kbase = (0 if t0 < QN else HC * QN)
                tl = t0 if t0 < QN else t0 - QN
                for ec in range(HC):
                    ins = t_eng.matmul(
                        psSs[widx % 4][:],
                        Q_sb[:, ec * QN + i * 128: ec * QN + (i + 1) * 128],
                        K_sb[:, kbase + ec * QN + tl: kbase + ec * QN + tl + 512],
                        start=(ec == 0), stop=(ec == HC - 1))
                score_ck[(i, w)] = tinc(ins)
                s_eng.wait_ge(smt, score_ck[(i, w)])
                if widx >= 2:
                    pi, pw = divmod(widx - 2, WIN)
                    s_eng.wait_ge(smv, thw_ck[(pi, pw)])
                s_eng.activation(t1s[widx % 2][:], psSs[widx % 4][:], AF.Tanh, scale=0.5)
                s_eng.activation(t2s[widx % 2][:], psSs[widx % 4][:], AF.Tanh)
                rl = s_eng.activation(theta[:, w * 512:(w + 1) * 512],
                                      psSs[widx % 4][:], AF.Relu, scale=0.1)
                lut_ck[(i, w)] = sinc(rl)
                v_eng.wait_ge(sms, lut_ck[(i, w)])
                if i >= 1 and w == 0:
                    v_eng.wait_ge(sms, exp_ck[i - 1])
                v_eng.scalar_tensor_tensor(t1s[widx % 2][:], t1s[widx % 2][:],
                                           2.0 / 3.0, t2s[widx % 2][:],
                                           ALU.mult, ALU.add)
                v_eng.drain()
                st2 = v_eng.scalar_tensor_tensor(theta[:, w * 512:(w + 1) * 512],
                                                 t1s[widx % 2][:], 0.15,
                                                 theta[:, w * 512:(w + 1) * 512],
                                                 ALU.mult, ALU.add)
                v_eng.drain()
                mx = v_eng.reduce_max(mcol[:, w:w + 1], theta[:, w * 512:(w + 1) * 512],
                                      axis=mybir.AxisListType.X)
                thw_ck[(i, w)] = vinc(mx)

        if j >= 0:
            t_eng.wait_ge(ldpt, 16 * (j + 1))
            if j >= 2:
                t_eng.wait_ge(sms, oev_ck[j - 2])
            ins = None
            if j == 0 and NVQ >= 2:
                bnd = 2 * VQ_BOUNDS[NVQ - 2]
                part1 = list(range(0, bnd)) + list(range(TC // 2, TC // 2 + bnd))
                part2 = [tci for tci in range(TC) if tci not in part1]
                for _sem, _val in dedup_waits(VL_DONE[:2 * (NVQ - 1)]):
                    t_eng.wait_ge(_sem, _val)
                for half in range(2):
                    for tci in part1:
                        t_eng.matmul(
                            psO2[0][:, half * 512:(half + 1) * 512],
                            PT_sb[:, tci * 128:(tci + 1) * 128],
                            V_sb[:, tci * D + half * 512: tci * D + (half + 1) * 512],
                            start=(tci == part1[0]), stop=False)
                for _sem, _val in dedup_waits(VL_DONE):
                    t_eng.wait_ge(_sem, _val)
                for half in range(2):
                    for tci in part2:
                        ins = t_eng.matmul(
                            psO2[0][:, half * 512:(half + 1) * 512],
                            PT_sb[:, tci * 128:(tci + 1) * 128],
                            V_sb[:, tci * D + half * 512: tci * D + (half + 1) * 512],
                            start=False, stop=(tci == part2[-1]))
            else:
                if j == 0:
                    for _sem, _val in dedup_waits(VL_DONE):
                        t_eng.wait_ge(_sem, _val)
                for half in range(2):
                    for tci in range(TC):
                        ins = t_eng.matmul(
                            psO2[j % 2][:, half * 512:(half + 1) * 512],
                            PT_sb[:, tci * 128:(tci + 1) * 128],
                            V_sb[:, tci * D + half * 512: tci * D + (half + 1) * 512],
                            start=(tci == 0), stop=(tci == TC - 1))
            av_ck[j] = tinc(ins)

        if i < TILES:
            v_eng.drain()
            v_eng.reduce_max(mrow[:], mcol[:, 0:WIN], axis=mybir.AxisListType.X)
            v_eng.drain()
            ng = v_eng.tensor_scalar(negm[:], mrow[:], -1.0, None, ALU.mult)
            negm_ck[i] = vinc(ng)
        if j >= 0:
            v_eng.wait_ge(sms, exp_ck[j])
            if j >= 1:
                v_eng.wait_ge(sms, oev_ck[j - 1])
            rc = v_eng.reciprocal(rrow[:], lrow[:])
            recip_ck[j] = vinc(rc)

        if i < TILES:
            if i >= 1:
                s_eng.wait_ge(smv, recip_ck[i - 1])
                s_eng.wait_ge(ldpt, 16 * i)
            else:
                s_eng.wait_ge(smv, negm_ck[0])
            ex = s_eng.activation(P_sb[:], theta[:], AF.Exp,
                                  bias=negm[:], scale=1.0, accum_out=lrow[:])
            exp_ck[i] = sinc(ex)
            s_eng.drain()
        if j >= 0:
            s_eng.wait_ge(smt, av_ck[j])
            s_eng.wait_ge(smv, recip_ck[j])
            if j >= 2:
                s_eng.wait_ge(stOns[j % 2], onst_at[j - 2])
            ev = s_eng.activation(Ons[j % 2][:], psO2[j % 2][:], AF.Identity,
                                  bias=0.0, scale=rrow[:])
            oev_ck[j] = sinc(ev)

        if j >= 0:
            q_eng.wait_ge(sms, oev_ck[j])
            q_eng.dma_start(out=on_scr[j * 128:(j + 1) * 128, :],
                            in_=Ons[j % 2][:]).then_inc(stOns[j % 2], 16)
            C[sok[j % 2]] += 16
            onst_at[j] = C[sok[j % 2]]
            q_eng.wait_ge(stOns[j % 2], onst_at[j])

    attn_end_s = C["s"]
    ST_ON_TOT = (C["so0"], C["so1"])
    s_at.close()
    s_v.close()
    s_kq.close()

    s_f = ExitStack()
    W1_sb = s_f.enter_context(nc.sbuf_tensor("W1_sb", [128, HC * INTER], BF16))
    W2_sb = s_f.enter_context(nc.sbuf_tensor("W2_sb", [128, IC * D], BF16))
    WoT = s_f.enter_context(nc.sbuf_tensor("WoT", [128, HC * D], BF16))
    H_sb = s_f.enter_context(nc.sbuf_tensor("H_sb", [128, IC * 256], BF16))
    x2nT2 = [s_f.enter_context(nc.sbuf_tensor(f"x2nT{i}", [128, HC * 256], BF16)) for i in range(2)]
    x2_2 = [s_f.enter_context(nc.sbuf_tensor(f"x2_{i}", [128, 2 * D], F32)) for i in range(2)]
    xr_sb = s_f.enter_context(nc.sbuf_tensor("xr_sb", [128, D], F32))
    On_sb = s_f.enter_context(nc.sbuf_tensor("On_sb", [128, D], BF16))
    y_sb = s_f.enter_context(nc.sbuf_tensor("y_sb", [128, D], F32))
    OT2 = [s_f.enter_context(nc.sbuf_tensor(f"OT{i}", [128, HC * 128], BF16)) for i in range(2)]
    xnF = s_f.enter_context(nc.sbuf_tensor("xnF", [128, D], BF16))
    psTf0 = s_f.enter_context(nc.psum_tensor("psTf0", [128, 1024], BF16))
    psTf1 = s_f.enter_context(nc.psum_tensor("psTf1", [128, 1024], BF16))
    psY = s_f.enter_context(nc.psum_tensor("psY", [128, 1024], F32))
    psH0 = s_f.enter_context(nc.psum_tensor("psH0", [128, 512], F32))
    psH1 = s_f.enter_context(nc.psum_tensor("psH1", [128, 512], F32))
    psY2 = s_f.enter_context(nc.psum_tensor("psY2", [128, 1024], F32))

    g_eng.wait_ge(smt, av_ck[TILES - 1])
    if SEQ < 4096:
        g_eng.wait_ge(sms, attn_end_s)
        g_eng.wait_ge(stOn0, ST_ON_TOT[0])
        g_eng.wait_ge(stOn1, ST_ON_TOT[1])
    go = gdma_start(H_sb[:, 0:HC * D].rearrange("p (e d) -> p e d", e=HC),
                    Wo.rearrange("(e p) d -> p e d", p=128))
    w1v = W1.rearrange("(q h p) i -> q p h i", q=2, p=128)
    w1_gd = []
    for qq in range(2):
        w1_gd.append(gdma_start(
            W1_sb[:, qq * 4 * INTER:(qq + 1) * 4 * INTER].rearrange("p (h i) -> p h i", h=4),
            w1v[qq]))
    w2v = W2.rearrange("(q ic p) h -> q p ic h", q=4, p=128)
    w2_gd = []
    for qq in range(4):
        w2_gd.append(gdma_start(
            W2_sb[:, qq * 8 * D:(qq + 1) * 8 * D].rearrange("p (ic h) -> p ic h", ic=8),
            w2v[qq]))
    wo_ck = emit_w_transpose(H_sb, WoT, go, [psTf0, psTf1])

    onv1 = on_scr.rearrange("(k p) d -> k p d", p=128)
    xrv1 = xb.rearrange("(k p) d -> k p d", p=128)

    ontr_ck = {}
    xntr_ck = {}
    otev_ck = {}
    yp_ck = {}
    x2_ck = {}
    x2nt_ck = {}
    h_ck = {}
    hev_ck = {}
    f2_ck = {}
    yv_ck = {}
    lon_at = {}
    lxr_at = {}
    apply_ck2 = {}
    psTfs = [psTf0, psTf1]
    psHs = [psH0, psH1]

    for u in range(FS + 1):
        if u < FS:
            xp = u % 2
            for sg in range(2):
                if (u, sg) == (0, 0):
                    q_eng.wait_ge(stOn0, ST_ON_TOT[0])
                    q_eng.wait_ge(stOn1, ST_ON_TOT[1])
                else:
                    psg = (u, sg - 1) if sg == 1 else (u - 1, 1)
                    q_eng.wait_ge(smt, ontr_ck[psg])
                q_eng.dma_start(out=On_sb[:], in_=onv1[u * 2 + sg]).then_inc(lOn, 16)
                C["lon"] += 16
                lon_at[(u, sg)] = C["lon"]
                t_eng.wait_ge(lOn, lon_at[(u, sg)])
                if u == 0 and sg == 0:
                    t_eng.wait_ge(sms, wo_ck)
                if u >= 1:
                    t_eng.wait_ge(sms, x2nt_ck[(u - 1, sg)])
                ins = None
                pst = psTfs[sg % 2]
                for hcc in range(HC):
                    ins = t_eng.transpose(pst[:, hcc * 128:(hcc + 1) * 128],
                                          On_sb[:, hcc * 128:(hcc + 1) * 128],
                                          eyesb[:])
                ontr_ck[(u, sg)] = tinc(ins)
                s_eng.wait_ge(smt, ontr_ck[(u, sg)])
                if u >= 1:
                    s_eng.wait_ge(smt, yp_ck[(u - 1, sg)])
                cp = s_eng.copy(OT2[sg][:], pst[:])
                otev_ck[(u, sg)] = sinc(cp)

            for sg in range(2):
                if (u, sg) != (0, 0):
                    psg = (u, sg - 1) if sg == 1 else (u - 1, 1)
                    q_eng.wait_ge(smv, x2_ck[psg])
                q_eng.dma_start(out=xr_sb[:], in_=xrv1[u * 2 + sg]).then_inc(lxr, 16)
                C["lxr"] += 16
                lxr_at[(u, sg)] = C["lxr"]
                t_eng.wait_ge(sms, otev_ck[(u, sg)])
                if (u, sg) != (0, 0):
                    psg = (u, sg - 1) if sg == 1 else (u - 1, 1)
                    t_eng.wait_ge(smv, x2_ck[psg])
                ins = None
                for half in range(2):
                    for hcc in range(HC):
                        t_eng.matmul(psY[:, half * 512:(half + 1) * 512],
                                     OT2[sg][:, hcc * 128:(hcc + 1) * 128],
                                     WoT[:, hcc * D + half * 512: hcc * D + (half + 1) * 512],
                                     start=(hcc == 0), stop=False)
                    ins = t_eng.matmul(psY[:, half * 512:(half + 1) * 512],
                                       ones_sb[0:1, :], bo_sb[0:1, half * 512:(half + 1) * 512],
                                       start=False, stop=True)
                yp_ck[(u, sg)] = tinc(ins)

                v_eng.wait_ge(smt, yp_ck[(u, sg)])
                v_eng.wait_ge(lxr, lxr_at[(u, sg)])
                if u >= 2 and sg == 0:
                    v_eng.wait_ge(sms, x2nt_ck[(u - 2, 1)])
                ad = v_eng.tensor_tensor(x2_2[xp][:, sg * D:(sg + 1) * D], psY[:],
                                         xr_sb[:], ALU.add)
                x2_ck[(u, sg)] = vinc(ad)

                if sg == 0:
                    apl = ln_chunk(x2_2[xp], 0, xnF,
                                   xntr_ck.get((u - 1, 1)),
                                   smv, x2_ck[(u, 0)])
                    apply_ck2[(u, 0)] = apl

        def emit_xntr(uu, sg, extra_scalar_wait=None):
            t_eng.wait_ge(sms, apply_ck2[(uu, sg)])
            ins = None
            pst = psTfs[sg % 2]
            for hcc in range(HC):
                ins = t_eng.transpose(pst[:, hcc * 128:(hcc + 1) * 128],
                                      xnF[:, hcc * 128:(hcc + 1) * 128], eyesb[:])
            tk = tinc(ins)
            xntr_ck[(uu, sg)] = tk
            if sg == 0:
                apl = ln_chunk(x2_2[uu % 2], D, xnF, tk, smv, x2_ck[(uu, 1)])
                apply_ck2[(uu, 1)] = apl
            s_eng.wait_ge(smt, tk)
            if extra_scalar_wait:
                s_eng.wait_ge(smt, extra_scalar_wait)
            cp = s_eng.copy(
                x2nT2[uu % 2].rearrange("p (h s) -> p h s", h=HC)[:, :, sg * 128:(sg + 1) * 128],
                pst.rearrange("p (h j) -> p h j", h=HC))
            x2nt_ck[(uu, sg)] = sinc(cp)

        if u == 0:
            emit_xntr(0, 0)
            emit_xntr(0, 1)

        if u >= 1:
            v = u - 1
            vp = v % 2
            t_eng.wait_ge(sms, x2nt_ck[(v, 1)])
            NP = IC // 2
            for icp in range(NP):
                if v == 0 and icp == 0:
                    t_eng.wait_ge(w1_gd[0][0], w1_gd[0][1])
                    t_eng.wait_ge(w1_gd[1][0], w1_gd[1][1])
                if icp >= 2:
                    t_eng.wait_ge(sms, hev_ck[(v, icp - 2)])
                elif v >= 1:
                    t_eng.wait_ge(sms, hev_ck[(v - 1, NP - 2 + icp)])
                ins = None
                for sub in range(2):
                    ic = 2 * icp + sub
                    for hcc in range(HC):
                        ins = t_eng.matmul(psHs[icp % 2][:, sub * 256:(sub + 1) * 256],
                                           W1_sb[:, hcc * INTER + ic * 128: hcc * INTER + (ic + 1) * 128],
                                           x2nT2[vp][:, hcc * 256:(hcc + 1) * 256],
                                           start=(hcc == 0), stop=(hcc == HC - 1))
                h_ck[(v, icp)] = tinc(ins)
                s_eng.wait_ge(smt, h_ck[(v, icp)])
                if v >= 1 and icp == 0:
                    s_eng.wait_ge(smt, f2_ck[(v - 1, 1)])
                ev = None
                for sub in range(2):
                    ic = 2 * icp + sub
                    ev = s_eng.activation(H_sb[:, ic * 256:(ic + 1) * 256],
                                          psHs[icp % 2][:, sub * 256:(sub + 1) * 256], AF.Relu,
                                          bias=b1_sb[:, ic:ic + 1], scale=1.0)
                hev_ck[(v, icp)] = sinc(ev)

            if u < FS:
                emit_xntr(u, 0, extra_scalar_wait=h_ck[(u - 1, IC // 2 - 1)])
                emit_xntr(u, 1)

            for sg in range(2):
                t_eng.wait_ge(sms, hev_ck[(v, IC // 2 - 1)])
                if (v, sg) != (0, 0):
                    psg = (v, sg - 1) if sg == 1 else (v - 1, 1)
                    t_eng.wait_ge(smv, yv_ck[psg])
                ins = None
                for half in range(2):
                    for ic in range(IC):
                        if v == 0 and sg == 0 and half == 0 and ic % 8 == 0:
                            t_eng.wait_ge(w2_gd[ic // 8][0], w2_gd[ic // 8][1])
                        t_eng.matmul(psY2[:, half * 512:(half + 1) * 512],
                                     H_sb[:, ic * 256 + sg * 128: ic * 256 + (sg + 1) * 128],
                                     W2_sb[:, ic * D + half * 512: ic * D + (half + 1) * 512],
                                     start=(ic == 0), stop=False)
                    ins = t_eng.matmul(psY2[:, half * 512:(half + 1) * 512],
                                       ones_sb[0:1, :], b2_sb[0:1, half * 512:(half + 1) * 512],
                                       start=False, stop=True)
                f2_ck[(v, sg)] = tinc(ins)
                v_eng.wait_ge(smt, f2_ck[(v, sg)])
                sidx = 2 * v + sg
                if sidx >= 1:
                    v_eng.wait_ge(sty, 16 * sidx)
                ad = v_eng.tensor_tensor(y_sb[:], psY2[:],
                                         x2_2[vp][:, sg * D:(sg + 1) * D], ALU.add)
                yv_ck[(v, sg)] = vinc(ad)
                s_eng.wait_ge(smv, yv_ck[(v, sg)])
                s_eng.dma_start(out=y[sidx * 128:(sidx + 1) * 128, :],
                                in_=y_sb[:]).then_inc(sty, 16)
                C["sty"] += 16

    q_eng.wait_ge(sty, C["sty"])

    s_f.close()
    top.close()
    return nc


_NC_CACHE = {}


def get_nc(SEQ=4096):
    if SEQ not in _NC_CACHE:
        _NC_CACHE[SEQ] = build_nc(SEQ)
    return _NC_CACHE[SEQ]


def core_inputs(x_full, weights, SEQ=4096):
    B = x_full.shape[0]
    eye_np = np.eye(128, dtype=BF)
    ones_np = np.ones((1, 128), dtype=BF)
    QN = SEQ // 2
    in_maps = []
    for c in range(2 * B):
        b, h = divmod(c, 2)
        xbat = x_full[b]
        xbp = np.ascontiguousarray(
            np.concatenate([xbat[h * QN:(h + 1) * QN], xbat[(1 - h) * QN:(2 - h) * QN]], axis=0))
        m = dict(weights)
        m["xb"] = xbp
        m["eye"] = eye_np
        m["ones"] = ones_np
        in_maps.append(m)
    return in_maps


def kernel(L0_attn_x, L0_attn_Wq, L0_attn_bq, L0_attn_Wk, L0_attn_bk,
           L0_attn_Wv, L0_attn_bv, L0_attn_Wo, L0_attn_bo,
           L0_ffn_W1, L0_ffn_b1, L0_ffn_W2, L0_ffn_b2):
    SEQ = 4096
    QN = SEQ // 2
    x = np.ascontiguousarray(np.asarray(L0_attn_x, dtype=np.float32))
    weights = dict(
        Wq=np.ascontiguousarray(np.asarray(L0_attn_Wq, dtype=np.float32)),
        Wk=np.ascontiguousarray(np.asarray(L0_attn_Wk, dtype=np.float32)),
        Wv=np.ascontiguousarray(np.asarray(L0_attn_Wv, dtype=np.float32)),
        Wo=np.ascontiguousarray(np.asarray(L0_attn_Wo, dtype=np.float32)),
        bq=np.ascontiguousarray(np.asarray(L0_attn_bq, dtype=np.float32)),
        bk=np.ascontiguousarray(np.asarray(L0_attn_bk, dtype=np.float32)),
        bv=np.ascontiguousarray(np.asarray(L0_attn_bv, dtype=np.float32)),
        bo=np.ascontiguousarray(np.asarray(L0_attn_bo, dtype=np.float32)),
        W1=np.ascontiguousarray(np.asarray(L0_ffn_W1, dtype=np.float32)),
        b1=np.ascontiguousarray(np.asarray(L0_ffn_b1, dtype=np.float32)),
        W2=np.ascontiguousarray(np.asarray(L0_ffn_W2, dtype=np.float32)),
        b2=np.ascontiguousarray(np.asarray(L0_ffn_b2, dtype=np.float32)),
    )
    nc = get_nc(SEQ)
    in_maps = core_inputs(x, weights, SEQ)
    res = run_bass_kernel_spmd(nc, in_maps, core_ids=list(range(8)))
    out = np.empty((4, SEQ, 1024), dtype=np.float32)
    for c in range(8):
        b, h = divmod(c, 2)
        out[b, h * QN:(h + 1) * QN] = res.results[c]["y"]
    return out
